# version 04
# baseline (speedup 1.0000x reference)
import math
from contextlib import ExitStack

import numpy as np

import concourse.bass as bass
import concourse.mybir as mybir
from concourse.bass_utils import run_bass_kernel_spmd

F32 = mybir.dt.float32
AF = mybir.ActivationFunctionType
ALU = mybir.AluOpType
AX = mybir.AxisListType

B, S_FULL, DM = 2, 8192, 1024
H, HD = 16, 64
NCORES = 8
HPC = H * B // NCORES
EPS_LN = 1e-5
PI2P1 = 1.0 + math.pi / 2.0
POW = -65.0
GEPS, WTHR = 1e-4, 0.1

CSCL = float(2.0 ** 69)
CONST_VALS = [PI2P1, POW, EPS_LN, -0.5, 0.5, 2.0, -2.0, 1.0 / DM, -float(DM),
              GEPS, WTHR, 99.0, -1e30, 0.25, CSCL, -4.0]


class Sched:

    ENGS = ("sync", "scalar", "vector", "tensor", "gpsimd")

    def __init__(self, nc, ctx):
        self.nc = nc
        self.ctx = ctx
        self.ops = []
        self.sems = {}
        self.step = {}
        self.count = {}

    def sem(self, name, step=1):
        if name not in self.sems:
            self.sems[name] = self.ctx.enter_context(self.nc.semaphore(name))
            self.step[name] = step
            self.count[name] = 0
        return name

    def cur(self, name):
        return self.count[name]

    def emit(self, eng, fn, waits=(), inc=None):
        wl = []
        for w in waits:
            if isinstance(w, tuple):
                wl.append(w)
            else:
                wl.append((w, self.count[w]))
        self.ops.append((eng, fn, wl, inc))
        if inc is not None:
            self.count[inc] += self.step[inc]
        return self.count[inc] if inc is not None else None

    def finalize(self, block):
        per = {e: [] for e in self.ENGS}
        for op in self.ops:
            per[op[0]].append(op)
        sched = self

        def make(engname, oplist):
            def body(eng):
                last = {}
                for _, fn, wl, inc in oplist:
                    for name, val in wl:
                        if val > 0 and last.get(name, -1) < val:
                            eng.wait_ge(sched.sems[name], val)
                            last[name] = val
                    instr = fn(eng)
                    if inc is not None:
                        instr.then_inc(sched.sems[inc], sched.step[inc])
                return None
            return body

        for engname in self.ENGS:
            if not per[engname]:
                continue
            getattr(block, engname)(make(engname, per[engname]))


def _consts_np():
    c = np.zeros((128, len(CONST_VALS)), np.float32)
    for j, v in enumerate(CONST_VALS):
        c[:, j] = v
    return c


def _register_consts(nc, cst_ap):
    for j, v in enumerate(CONST_VALS):
        nc.const_aps.register(F32, float(v), cst_ap[:, j:j + 1])




def build_attn(S=S_FULL, hpc=HPC, debug=False):
    NW = S // 512
    NR = S // 128
    GW = min(2048, S)
    NG = S // GW
    NC_ = S // 128

    nc = bass.Bass("TRN2", target_bir_lowering=False, debug=False)

    xfm = nc.dram_tensor("xfm", [DM, S], F32, kind="ExternalInput").ap()
    wqk = nc.dram_tensor("wqk", [hpc, DM, 128], F32, kind="ExternalInput").ap()
    wv = nc.dram_tensor("wv", [hpc, DM, 64], F32, kind="ExternalInput").ap()
    bqk = nc.dram_tensor("bqk", [hpc, 1, 128], F32, kind="ExternalInput").ap()
    bv = nc.dram_tensor("bv", [hpc, 1, 64], F32, kind="ExternalInput").ap()
    c_od = nc.dram_tensor("c_od", [128, 1], F32, kind="ExternalInput").ap()
    c_sel = nc.dram_tensor("c_sel", [128, 2], F32, kind="ExternalInput").ap()
    c_bc2 = nc.dram_tensor("c_bc2", [2, 128], F32, kind="ExternalInput").ap()
    c_o64 = nc.dram_tensor("c_o64", [1, 64], F32, kind="ExternalInput").ap()
    c_id = nc.dram_tensor("c_id", [128, 128], F32, kind="ExternalInput").ap()
    cst = nc.dram_tensor("cst", [128, len(CONST_VALS)], F32, kind="ExternalInput").ap()
    attn = nc.dram_tensor("attn", [hpc, 64, S], F32, kind="ExternalOutput").ap()
    if debug:
        dkst = nc.dram_tensor("dkst", [128, S], F32, kind="ExternalOutput").ap()
        dqk = nc.dram_tensor("dqk", [128, S], F32, kind="ExternalOutput").ap()
        dacc = nc.dram_tensor("dacc", [65, S], F32, kind="ExternalOutput").ap()
        dfrow = nc.dram_tensor("dfrow", [128, S], F32, kind="ExternalOutput").ap()
        dcs = nc.dram_tensor("dcs", [128, S // 128], F32, kind="ExternalOutput").ap()
        dvav = nc.dram_tensor("dvav", [128, S // 128], F32, kind="ExternalOutput").ap()

    with ExitStack() as ctx:
        E = ctx.enter_context
        AR = max(S, 8192)
        frow = E(nc.sbuf_tensor("frow", [128, AR], F32))
        zb = [E(nc.sbuf_tensor(f"zb{i}", [128, GW], F32)) for i in range(2)]
        cb = [E(nc.sbuf_tensor(f"cb{i}", [128, GW], F32)) for i in range(2)]
        qkhat = E(nc.sbuf_tensor("qkhat", [128, S], F32))
        kst = E(nc.sbuf_tensor("kst", [128, S], F32))
        khat = kst.tensor if False else kst
        xrow = E(nc.sbuf_tensor("xrow", [128, 1024], F32))
        acc = E(nc.sbuf_tensor("acc", [65, S], F32))
        augt = E(nc.sbuf_tensor("augt", [66, 192], F32))
        mstm = E(nc.sbuf_tensor("mstm", [128, 2 * NC_], F32))
        sstm = E(nc.sbuf_tensor("sstm", [128, NC_], F32))
        vtmp = E(nc.sbuf_tensor("vtmp", [128, NC_], F32))
        rstm = E(nc.sbuf_tensor("rstm", [128, NC_], F32))
        wsb = E(nc.sbuf_tensor("wsb", [128, 8 * 128], F32))
        wvsb = E(nc.sbuf_tensor("wvsb", [128, 8 * 64], F32))
        qksq = E(nc.sbuf_tensor("qksq", [128, 512], F32))
        qkc = E(nc.sbuf_tensor("qkc", [128, 512], F32))
        ln2b = E(nc.sbuf_tensor("ln2b", [2, 512], F32))
        scl2 = E(nc.sbuf_tensor("scl2", [2, 512], F32))
        vch = E(nc.sbuf_tensor("vch", [128, 64], F32))
        vav = E(nc.sbuf_tensor("vav", [128, 65], F32))
        colsum = E(nc.sbuf_tensor("colsum", [128, 1], F32))
        cj = E(nc.sbuf_tensor("cj", [128, 1], F32))
        sones = E(nc.sbuf_tensor("sones", [128, 1], F32))
        ssel = E(nc.sbuf_tensor("ssel", [128, 2], F32))
        sbc2 = E(nc.sbuf_tensor("sbc2", [2, 128], F32))
        so64t = E(nc.sbuf_tensor("so64t", [65, 64], F32))
        sid = E(nc.sbuf_tensor("sid", [128, 128], F32))
        scst = E(nc.sbuf_tensor("scst", [128, len(CONST_VALS)], F32))
        dstg = E(nc.sbuf_tensor("dstg", [64, 512], F32))
        dstg2 = E(nc.sbuf_tensor("dstg2", [64, 512], F32))
        if debug:
            dcsb = E(nc.sbuf_tensor("dcsb", [128, S // 128], F32))
            dvavb = E(nc.sbuf_tensor("dvavb", [128, S // 128], F32))
        PSA = E(nc.psum_tensor("PSA", [128, 2048], F32))
        PSB = E(nc.psum_tensor("PSB", [128, 2048], F32))

        _register_consts(nc, scst[:])

        sc = Sched(nc, ctx)
        s_ld = sc.sem("s_ld", 16)
        s_x = sc.sem("s_x", 16)
        s_t = sc.sem("s_t")
        s_v = sc.sem("s_v")
        s_s = sc.sem("s_s")
        s_g = sc.sem("s_g")
        s_kd = sc.sem("s_kd", 16)
        s_o = sc.sem("s_o", 16)
        s_xr = sc.sem("s_xr", 16)

        wr = {}
        rd = {}

        def W(buf):
            return list(rd.get(buf, [])) + list(wr.get(buf, []))

        def R(*bufs):
            out = []
            for b in bufs:
                out += list(wr.get(b, []))
            return out

        def setw(buf, sem):
            wr[buf] = [(sem, sc.cur(sem))]
            rd[buf] = []

        def addr(buf, sem):
            rd.setdefault(buf, [])
            rd[buf] = [x for x in rd[buf] if x[0] != sem] + [(sem, sc.cur(sem))]

        def fold(dst, *srcs):
            rd.setdefault(dst, [])
            for b in srcs:
                rd[dst] = rd[dst] + rd.pop(b, []) + wr.pop(b, [])

        for t, s, b in ((sones[:, 0:1], c_od[:], "sones"), (ssel[:], c_sel[:], "ssel"),
                        (sbc2[:], c_bc2[:], "sbc2"), (so64t[64:65, :], c_o64[:], "so64t"),
                        (sid[:], c_id[:], "sid"), (scst[:], cst[:], "scst")):
            def mk(t=t, s=s):
                return lambda e: e.dma_start(out=t, in_=s)
            sc.emit("sync", mk(), inc=s_ld)
        BAR = [(s_ld, sc.cur(s_ld))]

        xw = lambda par, c: frow[:, (par * 8 + c) * 512:(par * 8 + c) * 512 + 512]

        p_mu = PSB[0:1, 0:512]
        p_ss = PSB[0:1, 512:1024]
        p_t1 = PSB[:, 1024:1025]
        p_t2 = PSB[0:2, 1536:1664]
        for w in range(NW):
            par = w % 2
            for cch in range(8):
                def mk(w=w, cch=cch, par=par):
                    return lambda e: e.dma_start(
                        out=xw(par, cch),
                        in_=xfm[cch * 128:cch * 128 + 128, w * 512:w * 512 + 512])
                w0 = ([(s_x, sc.cur(s_x))] if cch == 0 else []) + (W(f"xw{par}") if cch == 0 else [])
                sc.emit("sync", mk(), waits=w0, inc=s_x)
            xv = sc.cur(s_x)
            setw(f"xw{par}", s_x)
            for cch in range(8):
                def mk(cch=cch, par=par):
                    return lambda e: e.matmul(p_mu, sones[:], xw(par, cch),
                                              start=(cch == 0), stop=(cch == 7))
                sc.emit("tensor", mk(), waits=BAR + [(s_x, xv)] + W("PB45"), inc=s_t)
            setw("PB45", s_t)
            for cch in range(8):
                def mkv(cch=cch, par=par):
                    return lambda e: e.tensor_mul(qksq[:], xw(par, cch), xw(par, cch))
                sc.emit("vector", mkv(), waits=[(s_x, xv)] + W("qksq"), inc=s_v)
                setw("qksq", s_v)
                def mkt(cch=cch):
                    return lambda e: e.matmul(p_ss, sones[:], qksq[:],
                                              start=(cch == 0), stop=(cch == 7))
                sc.emit("tensor", mkt(), waits=BAR + R("qksq") + (W("PB45") if cch == 0 else []),
                        inc=s_t)
                addr("qksq", s_t)
            setw("PB45", s_t)
            addr(f"xw{par}", s_t)
            addr(f"xw{par}", s_v)
            sc.emit("vector", lambda e: e.tensor_copy(qkc[0:1, 0:512], p_mu),
                    waits=R("PB45") + W("qkc"), inc=s_v)
            setw("qkc", s_v)
            addr("PB45", s_v)
            sc.emit("vector", lambda e: e.tensor_copy(qksq[0:1, 0:512], p_ss),
                    waits=R("PB45") + W("qksq"), inc=s_v)
            setw("qksq", s_v)
            addr("PB45", s_v)
            for t4 in range(4):
                cidx = w * 4 + t4
                def mk1(t4=t4):
                    return lambda e: e.transpose(p_t1, qkc[0:1, t4 * 128:t4 * 128 + 128],
                                                 sid[0:1, 0:1])
                sc.emit("tensor", mk1(), waits=BAR + R("qkc") + W("PB6"), inc=s_t)
                setw("PB6", s_t)
                addr("qkc", s_t)
                def mkc1(cidx=cidx):
                    return lambda e: e.tensor_copy(mstm[:, 2 * cidx:2 * cidx + 1], p_t1)
                sc.emit("vector", mkc1(), waits=R("PB6"), inc=s_v)
                addr("PB6", s_v)
                def mk2(t4=t4):
                    return lambda e: e.transpose(p_t1, qksq[0:1, t4 * 128:t4 * 128 + 128],
                                                 sid[0:1, 0:1])
                sc.emit("tensor", mk2(), waits=BAR + R("qksq") + W("PB6"), inc=s_t)
                setw("PB6", s_t)
                addr("qksq", s_t)
                def mkc2(cidx=cidx):
                    return lambda e: e.tensor_copy(sstm[:, cidx:cidx + 1], p_t1)
                sc.emit("vector", mkc2(), waits=R("PB6"), inc=s_v)
                addr("PB6", s_v)
        setw("mstm_mu", s_v)
        setw("sstm", s_v)
        mu_tm = mstm[:].rearrange("p (c t) -> p c t", t=2)
        sc.emit("vector", lambda e: e.tensor_mul(vtmp[:], mu_tm[:, :, 0], mu_tm[:, :, 0]),
                waits=R("mstm_mu"), inc=s_v)
        setw("vtmp", s_v)
        sc.emit("vector", lambda e: e.tensor_sub(sstm[:], sstm[:], vtmp[:]),
                waits=R("sstm", "vtmp"), inc=s_v)
        setw("sstm", s_v)
        sc.emit("scalar", lambda e: e.activation(sstm[:], sstm[:], AF.Ln,
                                                 bias=EPS_LN, scale=1.0),
                waits=BAR + R("sstm"), inc=s_s)
        setw("sstm", s_s)
        sc.emit("scalar", lambda e: e.activation(mu_tm[:, :, 1], sstm[:], AF.Exp, scale=0.5),
                waits=R("sstm"), inc=s_s)
        setw("mstm_sig", s_s)
        sc.emit("scalar", lambda e: e.activation(rstm[:], sstm[:], AF.Exp, scale=-0.5),
                waits=R("sstm"), inc=s_s)
        setw("rstm", s_s)
        addr("sstm", s_s)
        for c in range(NC_):
            def mk(c=c):
                return lambda e: e.transpose(p_t2, mstm[:, 2 * c:2 * c + 2], sid[:])
            sc.emit("tensor", mk(), waits=BAR + R("mstm_mu", "mstm_sig") + W("PB7"),
                    inc=s_t)
            setw("PB7", s_t)
            def mkc(c=c):
                return lambda e: e.tensor_copy(acc[0:2, c * 128:c * 128 + 128], p_t2)
            sc.emit("vector", mkc(), waits=R("PB7"), inc=s_v)
            addr("PB7", s_v)
        setw("acc02", s_v)
        addr("mstm_mu", s_t)
        addr("mstm_sig", s_t)
        sc.emit("gpsimd", lambda e: e.dma_start(out=kst[64:66, 0:S], in_=acc[0:2, 0:S]),
                waits=R("acc02") + [(s_kd, sc.cur(s_kd))], inc=s_kd)
        wr["stats"] = [(s_kd, sc.cur(s_kd))]
        rd["stats"] = []
        addr("acc02", s_kd)
        fold("acc", "acc02")

        for p in range(hpc):
            for cch in range(8):
                def mk(p=p, cch=cch):
                    return lambda e: e.dma_start(out=wsb[:, cch * 128:cch * 128 + 128],
                                                 in_=wqk[p, cch * 128:cch * 128 + 128, :])
                w0 = ([(s_ld, sc.cur(s_ld))] + W("wsb") + W("wvsb")) if cch == 0 else []
                sc.emit("sync", mk(), waits=w0, inc=s_ld)
            setw("wsb", s_ld)
            for cch in range(8):
                def mk(p=p, cch=cch):
                    return lambda e: e.dma_start(out=wvsb[:, cch * 64:cch * 64 + 64],
                                                 in_=wv[p, cch * 128:cch * 128 + 128, :])
                sc.emit("sync", mk(), inc=s_ld)
            setw("wvsb", s_ld)
            def mkb(p=p):
                return lambda e: e.dma_start(out=augt[65:66, 0:128], in_=bqk[p])
            sc.emit("sync", mkb(), waits=W("augw"), inc=s_ld)
            def mkb2(p=p):
                return lambda e: e.dma_start(out=augt[65:66, 128:192], in_=bv[p])
            sc.emit("sync", mkb2(), waits=W("augv"), inc=s_ld)
            wld = sc.cur(s_ld)
            wr["augw"] = [(s_ld, wld)]
            wr["augv"] = [(s_ld, wld)]
            rd["augw"] = []
            rd["augv"] = []
            wr["wsb"] = [(s_ld, wld)]
            wr["wvsb"] = [(s_ld, wld)]
            p_sw = PSA[0:1, 0:128]
            for cch in range(8):
                def mk(cch=cch):
                    return lambda e: e.matmul(p_sw, sones[:],
                                              wsb[:, cch * 128:cch * 128 + 128],
                                              start=(cch == 0), stop=(cch == 7))
                sc.emit("tensor", mk(), waits=BAR + R("wsb") + (W("PSA") if cch == 0 else []),
                        inc=s_t)
            setw("PSA", s_t)
            sc.emit("vector", lambda e: e.tensor_scalar(acc[0:1, 0:128], p_sw, -float(DM),
                                                        None, op0=ALU.mult),
                    waits=R("PSA", "augw") + W("augw_r0") + W("acc"), inc=s_v)
            setw("augw_r0", s_v)
            addr("PSA", s_v)
            p_swv = PSA[0:1, 512:576]
            for cch in range(8):
                def mk(cch=cch):
                    return lambda e: e.matmul(p_swv, sones[:],
                                              wvsb[:, cch * 64:cch * 64 + 64],
                                              start=(cch == 0), stop=(cch == 7))
                sc.emit("tensor", mk(), waits=BAR + R("wvsb") + (W("PSA") if cch == 0 else []),
                        inc=s_t)
            setw("PSA", s_t)
            sc.emit("vector", lambda e: e.tensor_scalar(acc[0:1, 128:192], p_swv, -float(DM),
                                                        None, op0=ALU.mult),
                    waits=R("PSA", "augv") + W("augv_r0") + W("acc"), inc=s_v)
            setw("augv_r0", s_v)
            addr("PSA", s_v)
            sc.emit("gpsimd", lambda e: e.dma_start(out=augt[64:65, 0:192],
                                                    in_=acc[0:1, 0:192]),
                    waits=[(s_kd, sc.cur(s_kd))] + R("augw_r0", "augv_r0") +
                          list(rd.get("augw", [])) + list(rd.get("augv", [])),
                    inc=s_kd)
            wr["augw_r0"] = [(s_kd, sc.cur(s_kd))]
            wr["augv_r0"] = [(s_kd, sc.cur(s_kd))]
            addr("acc", s_kd)
            aug_ready = R("augw_r0", "augv_r0", "augw", "augv")

            fold("xw0", "frow")
            rd["xw1"] = rd.get("xw1", []) + list(rd.get("xw0", []))
            p_qk = PSB[:, 1536:2048]
            p_ss2 = PSB[0:2, 1024:1536]
            p_bc = PSB[:, 0:512]
            for w in range(NW):
                par = w % 2
                for cch in range(8):
                    def mk(w=w, cch=cch, par=par):
                        return lambda e: e.dma_start(
                            out=xw(par, cch),
                            in_=xfm[cch * 128:cch * 128 + 128, w * 512:w * 512 + 512])
                    w0 = ([(s_x, sc.cur(s_x))] + W(f"xw{par}")) if cch == 0 else []
                    sc.emit("sync", mk(), waits=w0, inc=s_x)
                xv = sc.cur(s_x)
                setw(f"xw{par}", s_x)
                for cch in range(8):
                    def mk(cch=cch, par=par):
                        return lambda e: e.matmul(p_qk, wsb[:, cch * 128:cch * 128 + 128],
                                                  xw(par, cch), start=(cch == 0), stop=False)
                    ww = ([(s_x, xv)] + aug_ready + R("wsb") + W("PB7")) if cch == 0 else []
                    sc.emit("tensor", mk(), waits=ww, inc=s_t)
                def mkaug(w=w):
                    return lambda e: e.matmul(p_qk, augt[64:66, 0:128],
                                              kst[64:66, w * 512:w * 512 + 512],
                                              start=False, stop=True)
                sc.emit("tensor", mkaug(), waits=R("stats"), inc=s_t)
                setw("PB7", s_t)
                addr(f"xw{par}", s_t)
                sc.emit("vector", lambda e: e.tensor_copy(qkc[:], p_qk),
                        waits=R("PB7") + W("qkc"), inc=s_v)
                setw("qkc", s_v)
                sc.emit("vector", lambda e: e.tensor_mul(qksq[:], qkc[:], p_qk),
                        waits=R("PB7", "qkc") + W("qksq"), inc=s_v)
                setw("qksq", s_v)
                addr("PB7", s_v)
                sc.emit("tensor", lambda e: e.matmul(p_ss2, ssel[:], qksq[:],
                                                     start=True, stop=True),
                        waits=BAR + R("qksq") + W("PB6"), inc=s_t)
                setw("PB6", s_t)
                addr("qksq", s_t)
                sc.emit("scalar", lambda e: e.activation(ln2b[:], p_ss2, AF.Ln),
                        waits=BAR + R("PB6") + W("ln2b"), inc=s_s)
                setw("ln2b", s_s)
                addr("PB6", s_s)
                sc.emit("scalar", lambda e: e.activation(scl2[:], ln2b[:], AF.Exp,
                                                         scale=-0.5),
                        waits=R("ln2b") + W("scl2"), inc=s_s)
                setw("scl2", s_s)
                addr("ln2b", s_s)
                sc.emit("tensor", lambda e: e.matmul(p_bc, sbc2[:], scl2[:],
                                                     start=True, stop=True),
                        waits=BAR + R("scl2") + W("PB45"), inc=s_t)
                setw("PB45", s_t)
                addr("scl2", s_t)
                def mknrm(w=w):
                    return lambda e: e.tensor_mul(qkhat[:, w * 512:w * 512 + 512],
                                                  qkc[:], p_bc)
                sc.emit("vector", mknrm(), waits=R("PB45", "qkc") + W(f"qkhat{w}"), inc=s_v)
                setw(f"qkhat{w}", s_v)
                addr("PB45", s_v)
                addr("qkc", s_v)
                def mkkd(w=w):
                    return lambda e: e.dma_start(out=kst[0:64, w * 512:w * 512 + 512],
                                                 in_=qkhat[64:128, w * 512:w * 512 + 512])
                w0 = [(s_kd, sc.cur(s_kd))] if w == 0 else []
                sc.emit("gpsimd", mkkd(), waits=w0 + R(f"qkhat{w}") + W(f"khat{w}"),
                        inc=s_kd)
                rd[f"khat{w}"] = []
            kd_end = sc.cur(s_kd)
            for w2 in range(NW):
                wr[f"khat{w2}"] = [(s_kd, kd_end)]
                rd.setdefault(f"qkhat{w2}", []).append((s_kd, kd_end))
            qk_ready = [(s_kd, kd_end), (s_v, sc.cur(s_v))]

            fold("frow", "xw0", "xw1")
            sc.emit("vector", lambda e: e.memset(acc[:], 0.0), waits=W("acc"), inc=s_v)
            setw("acc", s_v)
            for r in range(NR):
                for cch in range(8):
                    def mk(r=r, cch=cch):
                        return lambda e: e.dma_start(
                            out=xrow[:, cch * 128:cch * 128 + 128],
                            in_=xfm[cch * 128:cch * 128 + 128, r * 128:r * 128 + 128])
                    w0 = ([(s_xr, sc.cur(s_xr))] + W("xrow")) if cch == 0 else []
                    sc.emit("sync", mk(), waits=w0, inc=s_xr)
                xrv = sc.cur(s_xr)
                setw("xrow", s_xr)
                p_v = PSB[:, 1024:1088]
                for cch in range(8):
                    def mk(cch=cch):
                        return lambda e: e.matmul(p_v, xrow[:, cch * 128:cch * 128 + 128],
                                                  wvsb[:, cch * 64:cch * 64 + 64],
                                                  start=(cch == 0), stop=False)
                    ww = ([(s_xr, xrv)] + R("wvsb") + W("PB6")) if cch == 0 else []
                    sc.emit("tensor", mk(), waits=ww, inc=s_t)
                def mkvaug(r=r):
                    return lambda e: e.matmul(p_v, kst[64:66, r * 128:r * 128 + 128],
                                              augt[64:66, 128:192], start=False, stop=True)
                sc.emit("tensor", mkvaug(), waits=aug_ready + R("stats"), inc=s_t)
                setw("PB6", s_t)
                addr("xrow", s_t)
                def mkvev(r=r):
                    return lambda e: e.activation(vch[:], p_v, AF.Copy,
                                                  scale=rstm[:, r:r + 1])
                sc.emit("scalar", mkvev(), waits=R("PB6", "rstm") + W("vch"), inc=s_s)
                setw("vch", s_s)
                addr("PB6", s_s)

                for g in range(NG):
                    gpar = g % 2
                    for wi in range(GW // 512):
                        w = g * (GW // 512) + wi
                        def mk(r=r, w=w, wi=wi):
                            return lambda e: e.matmul(
                                PSA[:, wi * 512:wi * 512 + 512],
                                kst[0:64, r * 128:r * 128 + 128],
                                qkhat[0:64, w * 512:w * 512 + 512],
                                start=True, stop=True)
                        sc.emit("tensor", mk(), waits=qk_ready + (W("PSA") if wi == 0 else []),
                                inc=s_t)
                    setw("PSA", s_t)
                    def mkl1(gpar=gpar):
                        return lambda e: e.activation(cb[gpar][:], PSA[:, 0:GW], AF.Ln,
                                                      bias=0.5, scale=0.5)
                    sc.emit("scalar", mkl1(), waits=R("PSA") + W(f"cb{gpar}"), inc=s_s)
                    setw(f"cb{gpar}", s_s)
                    def mkl2(gpar=gpar):
                        return lambda e: e.activation(zb[gpar][:], PSA[:, 0:GW], AF.Ln,
                                                      bias=0.5, scale=-0.5)
                    sc.emit("scalar", mkl2(), waits=R("PSA") + W(f"zb{gpar}"), inc=s_s)
                    setw(f"zb{gpar}", s_s)
                    addr("PSA", s_s)
                    def mke1(gpar=gpar):
                        return lambda e: e.activation(cb[gpar][:], cb[gpar][:], AF.Exp,
                                                      scale=0.5)
                    sc.emit("scalar", mke1(), waits=R(f"cb{gpar}"), inc=s_s)
                    setw(f"cb{gpar}", s_s)
                    def mke2(gpar=gpar):
                        return lambda e: e.activation(zb[gpar][:], zb[gpar][:], AF.Exp,
                                                      scale=-0.5)
                    sc.emit("scalar", mke2(), waits=R(f"zb{gpar}"), inc=s_s)
                    setw(f"zb{gpar}", s_s)
                    def mku(g=g, gpar=gpar):
                        return lambda e: e.scalar_tensor_tensor(
                            frow[:, g * GW:g * GW + GW], cb[gpar][:], -1.0, zb[gpar][:],
                            op0=ALU.add, op1=ALU.mult)
                    sc.emit("vector", mku(), waits=R(f"zb{gpar}", f"cb{gpar}") + W("frow"),
                            inc=s_v)
                    if g == NG - 1:
                        setw("frow", s_v)
                    addr(f"zb{gpar}", s_v)
                    addr(f"cb{gpar}", s_v)
                sc.emit("scalar", lambda e: e.activation(frow[:, 0:S], frow[:, 0:S],
                                                         AF.Arctan),
                        waits=R("frow"), inc=s_s)
                setw("frow", s_s)
                sc.emit("scalar", lambda e: e.activation(frow[:, 0:S], frow[:, 0:S], AF.Ln,
                                                         bias=1.0, scale=-4.0),
                        waits=R("frow"), inc=s_s)
                setw("frow", s_s)
                def mkexp():
                    return lambda e: e.activation(frow[:, 0:S], frow[:, 0:S], AF.Exp,
                                                  scale=POW, accum_out=colsum[:])
                sc.emit("scalar", mkexp(), waits=R("frow") + W("colsum"), inc=s_s)
                setw("frow", s_s)
                setw("colsum", s_s)
                if debug and p == 0:
                    def mkdc(r=r):
                        return lambda e: e.tensor_copy(dcsb[:, r:r + 1], colsum[:])
                    sc.emit("vector", mkdc(), waits=R("colsum") + [(s_v, sc.cur(s_v))],
                            inc=s_v)
                sc.emit("scalar", lambda e: e.activation(cj[:], colsum[:], AF.Ln,
                                                         scale=CSCL),
                        waits=R("colsum") + W("cj"), inc=s_s)
                setw("cj", s_s)
                addr("colsum", s_s)
                sc.emit("scalar", lambda e: e.activation(cj[:], cj[:], AF.Exp, scale=-0.5),
                        waits=R("cj"), inc=s_s)
                setw("cj", s_s)
                def mkva():
                    return lambda e: e.tensor_scalar(vav[:, 0:64], vch[:], cj[:, 0:1],
                                                     None, op0=ALU.mult)
                sc.emit("vector", mkva(), waits=R("cj", "vch") + W("vav"), inc=s_v)
                sc.emit("vector", lambda e: e.tensor_copy(vav[:, 64:65], cj[:]),
                        waits=R("cj"), inc=s_v)
                if debug and p == 0:
                    def mkdv(r=r):
                        return lambda e: e.tensor_copy(dvavb[:, r:r + 1], cj[:])
                    sc.emit("vector", mkdv(), waits=R("cj") + [(s_v, sc.cur(s_v))], inc=s_v)
                setw("vav", s_v)
                addr("cj", s_v)
                addr("vch", s_v)
                for wp in range(NW // 2):
                    for half in range(2):
                        w = wp * 2 + half
                        def mk(w=w, half=half):
                            return lambda e: e.matmul(
                                PSB[0:65, half * 512:half * 512 + 512],
                                vav[:], frow[:, w * 512:w * 512 + 512],
                                start=True, stop=True)
                        sc.emit("tensor", mk(), waits=R("vav", "frow") +
                                (W("PB45") if half == 0 else []), inc=s_t)
                    setw("PB45", s_t)
                    if wp == NW // 2 - 1:
                        addr("frow", s_t)
                        addr("vav", s_t)
                    def mkacc(wp=wp):
                        return lambda e: e.tensor_add(acc[:, wp * 1024:wp * 1024 + 1024],
                                                      acc[:, wp * 1024:wp * 1024 + 1024],
                                                      PSB[0:65, 0:1024])
                    sc.emit("vector", mkacc(), waits=R("PB45", "acc"), inc=s_v)
                    addr("PB45", s_v)
                    wr["acc"] = [(s_v, sc.cur(s_v))]

            if debug and p == 0:
                for t_, s_ in (("dkst", kst[:, 0:S]), ("dqk", qkhat[:, 0:S]),
                               ("dacc", acc[:, 0:S]), ("dfrow", frow[:, 0:S]),
                               ("dcs", dcsb[:]), ("dvav", dvavb[:])):
                    def mkd(t_=t_, s_=s_):
                        tgt = {"dkst": dkst, "dqk": dqk, "dacc": dacc, "dfrow": dfrow,
                               "dcs": dcs, "dvav": dvav}[t_]
                        return lambda e: e.dma_start(out=tgt[:], in_=s_)
                    sc.emit("gpsimd", mkd(), waits=[(s_o, sc.cur(s_o)), (s_v, sc.cur(s_v)),
                                                    (s_s, sc.cur(s_s)), (s_t, sc.cur(s_t)),
                                                    (s_g, sc.cur(s_g))], inc=s_o)

            for w in range(NW):
                p_db = PSB[0:64, 0:512]
                def mkb(w=w):
                    return lambda e: e.matmul(p_db, so64t[64:65, :],
                                              acc[64:65, w * 512:w * 512 + 512],
                                              start=True, stop=True)
                sc.emit("tensor", mkb(), waits=BAR + R("acc") + W("PB45"), inc=s_t)
                setw("PB45", s_t)
                addr("acc", s_t)
                def mkr(w=w):
                    return lambda e: e.reciprocal(dstg2[:], p_db)
                sc.emit("vector", mkr(), waits=R("PB45") + W("dstg2"), inc=s_v)
                setw("dstg2", s_v)
                addr("PB45", s_v)
                def mkm(w=w):
                    return lambda e: e.tensor_mul(dstg[:], acc[0:64, w * 512:w * 512 + 512],
                                                  dstg2[:])
                sc.emit("vector", mkm(), waits=R("dstg2", "acc") + W("dstg"), inc=s_v)
                setw("dstg", s_v)
                addr("dstg2", s_v)
                def mko(p=p, w=w):
                    return lambda e: e.dma_start(out=attn[p, :, w * 512:w * 512 + 512],
                                                 in_=dstg[:])
                sc.emit("gpsimd", mko(), waits=[(s_o, sc.cur(s_o))] + R("dstg"), inc=s_o)
                addr("dstg", s_o)
            addr("acc", s_t)
            for w2 in range(NW):
                addr(f"khat{w2}", s_t)
                addr(f"qkhat{w2}", s_t)
            for bname in ("wsb", "wvsb", "augw", "augv", "augw_r0", "augv_r0"):
                addr(bname, s_t)
            addr("wvsb", s_s)

        sc.emit("gpsimd", lambda e: e.nop(), waits=[(s_o, sc.cur(s_o))])

        block = E(nc.Block())
        sc.finalize(block)
    return nc



def build_tail(TOK=S_FULL * B // NCORES, BLK=512):
    NB = TOK // BLK
    BT = BLK // 128
    NE = 8

    nc = bass.Bass("TRN2", target_bir_lowering=False, debug=False)

    afm = nc.dram_tensor("afm", [DM, TOK], F32, kind="ExternalInput").ap()
    xtm = nc.dram_tensor("xtm", [TOK, DM], F32, kind="ExternalInput").ap()
    wo = nc.dram_tensor("wo", [DM, DM], F32, kind="ExternalInput").ap()
    ob = nc.dram_tensor("ob", [1, DM], F32, kind="ExternalInput").ap()
    gw = nc.dram_tensor("gw", [DM, 8], F32, kind="ExternalInput").ap()
    gb = nc.dram_tensor("gb", [1, 8], F32, kind="ExternalInput").ap()
    ew = nc.dram_tensor("ew", [8, DM, DM], F32, kind="ExternalInput").ap()
    eb = nc.dram_tensor("eb", [8, 1, DM], F32, kind="ExternalInput").ap()
    c_od = nc.dram_tensor("c_od", [128, 1], F32, kind="ExternalInput").ap()
    c_o128 = nc.dram_tensor("c_o128", [1, 128], F32, kind="ExternalInput").ap()
    c_id = nc.dram_tensor("c_id", [128, 128], F32, kind="ExternalInput").ap()
    c_iota = nc.dram_tensor("c_iota", [128, 8], F32, kind="ExternalInput").ap()
    c_io99 = nc.dram_tensor("c_io99", [128, 8], F32, kind="ExternalInput").ap()
    cst = nc.dram_tensor("cst", [128, len(CONST_VALS)], F32, kind="ExternalInput").ap()
    yout = nc.dram_tensor("yout", [TOK, DM], F32, kind="ExternalOutput").ap()

    with ExitStack() as ctx:
        E = ctx.enter_context
        wo_sb = E(nc.sbuf_tensor("wo_sb", [128, 8 * DM], F32))
        gw_sb = E(nc.sbuf_tensor("gw_sb", [128, 8 * 8], F32))
        gaug = E(nc.sbuf_tensor("gaug", [2, 8], F32))
        ob_sb = E(nc.sbuf_tensor("ob_sb", [1, DM], F32))
        onr = E(nc.sbuf_tensor("onr", [1, 128], F32))
        afm_t = E(nc.sbuf_tensor("afm_t", [128, 2 * DM], F32))
        xt_t = E(nc.sbuf_tensor("xt_t", [128, 2 * DM], F32))
        r2tm = E(nc.sbuf_tensor("r2tm", [128, BT * DM], F32))
        r2fm = E(nc.sbuf_tensor("r2fm", [128, 8 * BLK], F32))
        accb = E(nc.sbuf_tensor("accb", [128, BT * DM], F32))
        web = [E(nc.sbuf_tensor(f"web{i}", [128, 8 * DM], F32)) for i in range(2)]
        statsW = E(nc.sbuf_tensor("statsW", [2, 8 * DM], F32))
        statsT = E(nc.sbuf_tensor("statsT", [2, BLK], F32))
        sqb = E(nc.sbuf_tensor("sqb", [128, DM], F32))
        msi = E(nc.sbuf_tensor("msi", [128, 2 * BT], F32))
        sss = E(nc.sbuf_tensor("sss", [128, BT], F32))
        vt2 = E(nc.sbuf_tensor("vt2", [128, BT], F32))
        rsg = E(nc.sbuf_tensor("rsg", [128, BT], F32))
        gsc = E(nc.sbuf_tensor("gsc", [128, 8 * BT], F32))
        comb = E(nc.sbuf_tensor("comb", [128, 8 * BT], F32))
        tk = E(nc.sbuf_tensor("tk", [128, 8 * 6], F32))
        tk1 = E(nc.sbuf_tensor("tk1", [128, 8], F32))
        siota = E(nc.sbuf_tensor("siota", [128, 8], F32))
        sio99 = E(nc.sbuf_tensor("sio99", [128, 8], F32))
        sones = E(nc.sbuf_tensor("sones", [128, 1], F32))
        sid = E(nc.sbuf_tensor("sid", [128, 128], F32))
        scst = E(nc.sbuf_tensor("scst", [128, len(CONST_VALS)], F32))
        PSA = E(nc.psum_tensor("PSA", [128, 2048], F32))
        PSB = E(nc.psum_tensor("PSB", [128, 2048], F32))

        _register_consts(nc, scst[:])

        sc = Sched(nc, ctx)
        s_ld = sc.sem("s_ld", 16)
        s_we = sc.sem("s_we", 16)
        s_t = sc.sem("s_t")
        s_v = sc.sem("s_v")
        s_s = sc.sem("s_s")
        s_o = sc.sem("s_o", 16)

        wr = {}
        rd = {}

        def W(buf):
            return list(rd.get(buf, [])) + list(wr.get(buf, []))

        def R(*bufs):
            out = []
            for b2 in bufs:
                out += list(wr.get(b2, []))
            return out

        def setw(buf, sem):
            wr[buf] = [(sem, sc.cur(sem))]
            rd[buf] = []

        def addr(buf, sem):
            rd.setdefault(buf, [])
            rd[buf] = [x for x in rd[buf] if x[0] != sem] + [(sem, sc.cur(sem))]

        for t, s in ((sones[:], c_od[:]), (onr[:], c_o128[:]), (sid[:], c_id[:]),
                     (siota[:], c_iota[:]), (sio99[:], c_io99[:]), (scst[:], cst[:]), (ob_sb[:], ob[:]),
                     (gaug[1:2, :], gb[:])):
            def mk(t=t, s=s):
                return lambda e: e.dma_start(out=t, in_=s)
            sc.emit("sync", mk(), inc=s_ld)
        for cch in range(8):
            def mk(cch=cch):
                return lambda e: e.dma_start(out=wo_sb[:, cch * DM:cch * DM + DM],
                                             in_=wo[cch * 128:cch * 128 + 128, :])
            sc.emit("sync", mk(), inc=s_ld)
        for cch in range(8):
            def mk(cch=cch):
                return lambda e: e.dma_start(out=gw_sb[:, cch * 8:cch * 8 + 8],
                                             in_=gw[cch * 128:cch * 128 + 128, :])
            sc.emit("sync", mk(), inc=s_ld)
        for e_ in range(8):
            def mk(e_=e_):
                return lambda e: e.dma_start(out=statsW[1:2, e_ * DM:e_ * DM + DM],
                                             in_=eb[e_])
            sc.emit("sync", mk(), inc=s_ld)
        BAR = [(s_ld, sc.cur(s_ld))]
        p_g = PSB[0:1, 0:8]
        for cch in range(8):
            def mk(cch=cch):
                return lambda e: e.matmul(p_g, sones[:], gw_sb[:, cch * 8:cch * 8 + 8],
                                          start=(cch == 0), stop=(cch == 7))
            sc.emit("tensor", mk(), waits=BAR, inc=s_t)
        setw("PB45", s_t)
        sc.emit("vector", lambda e: e.tensor_scalar(gaug[0:1, :], p_g, -float(DM),
                                                    None, op0=ALU.mult),
                waits=R("PB45") + BAR, inc=s_v)
        setw("gaug0", s_v)
        addr("PB45", s_v)

        def we_load(e_, buf):
            for cch in range(8):
                def mk(e_=e_, cch=cch, buf=buf):
                    return lambda e: e.dma_start(out=web[buf][:, cch * DM:cch * DM + DM],
                                                 in_=ew[e_, cch * 128:cch * 128 + 128, :])
                w0 = ([(s_we, sc.cur(s_we))] + W(f"web{buf}")) if cch == 0 else []
                sc.emit("sync", mk(), waits=w0, inc=s_we)
            setw(f"web{buf}", s_we)

        we_load(0, 0)
        nstream = 1

        for b in range(NB):
            t0 = b * BLK
            for tt in range(BT):
                slot = tt % 2
                for cch in range(8):
                    def mk(tt=tt, cch=cch, t0=t0, slot=slot):
                        return lambda e: e.dma_start(
                            out=afm_t[:, slot * DM + cch * 128:slot * DM + cch * 128 + 128],
                            in_=afm[cch * 128:cch * 128 + 128,
                                    t0 + tt * 128:t0 + tt * 128 + 128])
                    w0 = ([(s_ld, sc.cur(s_ld))] + W(f"afm{slot}")) if cch == 0 else []
                    sc.emit("sync", mk(), waits=w0, inc=s_ld)
                def mkx(tt=tt, t0=t0, slot=slot):
                    return lambda e: e.dma_start(
                        out=xt_t[:, slot * DM:slot * DM + DM],
                        in_=xtm[t0 + tt * 128:t0 + tt * 128 + 128, :])
                sc.emit("sync", mkx(), waits=W(f"xt{slot}"), inc=s_ld)
                setw(f"afm{slot}", s_ld)
                setw(f"xt{slot}", s_ld)
                for half in range(2):
                    p_y = PSA[:, half * 512:half * 512 + 512]
                    for cch in range(8):
                        def mk(half=half, cch=cch, p_y=p_y, slot=slot):
                            return lambda e: e.matmul(
                                p_y,
                                afm_t[:, slot * DM + cch * 128:slot * DM + cch * 128 + 128],
                                wo_sb[:, cch * DM + half * 512:cch * DM + half * 512 + 512],
                                start=(cch == 0), stop=False)
                        ww = (R(f"afm{slot}") + BAR + W("PSA01")) if cch == 0 else []
                        sc.emit("tensor", mk(), waits=ww, inc=s_t)
                    def mka(half=half, p_y=p_y):
                        return lambda e: e.matmul(
                            p_y, onr[:], ob_sb[:, half * 512:half * 512 + 512],
                            start=False, stop=True)
                    sc.emit("tensor", mka(), inc=s_t)
                setw("PSA01", s_t)
                addr(f"afm{slot}", s_t)
                def mkev(tt=tt, slot=slot):
                    return lambda e: e.tensor_add(r2tm[:, tt * DM:tt * DM + DM],
                                                  xt_t[:, slot * DM:slot * DM + DM],
                                                  PSA[:, 0:1024])
                sc.emit("vector", mkev(), waits=R("PSA01", f"xt{slot}") +
                        (W("r2tm") if tt == 0 else []), inc=s_v)
                addr("PSA01", s_v)
                addr(f"xt{slot}", s_v)
            setw("r2tm", s_v)

            for tt in range(BT):
                def mkmu(tt=tt):
                    return lambda e: e.tensor_reduce(
                        msi[:, 2 * tt:2 * tt + 1], r2tm[:, tt * DM:tt * DM + DM],
                        axis=AX.X, op=ALU.add)
                sc.emit("vector", mkmu(), waits=R("r2tm") + (W("msi") if tt == 0 else []),
                        inc=s_v)
                def mksq(tt=tt):
                    return lambda e: e.tensor_mul(sqb[:], r2tm[:, tt * DM:tt * DM + DM],
                                                  r2tm[:, tt * DM:tt * DM + DM])
                sc.emit("vector", mksq(), waits=R("r2tm") + W("sqb"), inc=s_v)
                setw("sqb", s_v)
                def mkss(tt=tt):
                    return lambda e: e.tensor_reduce(sss[:, tt:tt + 1], sqb[:],
                                                     axis=AX.X, op=ALU.add)
                sc.emit("vector", mkss(), waits=R("sqb") + (W("sss") if tt == 0 else []),
                        inc=s_v)
                addr("sqb", s_v)
            mu_v = msi[:].rearrange("p (c t) -> p c t", t=2)
            sc.emit("vector", lambda e: e.tensor_scalar(mu_v[:, :, 0], mu_v[:, :, 0],
                                                        1.0 / DM, None, op0=ALU.mult),
                    waits=[(s_v, sc.cur(s_v))], inc=s_v)
            sc.emit("vector", lambda e: e.tensor_scalar(sss[:], sss[:], 1.0 / DM,
                                                        None, op0=ALU.mult),
                    waits=[(s_v, sc.cur(s_v))], inc=s_v)
            sc.emit("vector", lambda e: e.tensor_mul(vt2[:], mu_v[:, :, 0], mu_v[:, :, 0]),
                    waits=[(s_v, sc.cur(s_v))] + W("vt2"), inc=s_v)
            setw("vt2", s_v)
            sc.emit("vector", lambda e: e.tensor_sub(sss[:], sss[:], vt2[:]),
                    waits=[(s_v, sc.cur(s_v))], inc=s_v)
            setw("sss", s_v)
            sc.emit("scalar", lambda e: e.activation(sss[:], sss[:], AF.Ln,
                                                     bias=EPS_LN, scale=1.0),
                    waits=BAR + R("sss") + [(s_s, sc.cur(s_s))], inc=s_s)
            setw("sss", s_s)
            sc.emit("scalar", lambda e: e.activation(mu_v[:, :, 1], sss[:], AF.Exp,
                                                     scale=0.5),
                    waits=R("sss") + [(s_s, sc.cur(s_s))], inc=s_s)
            setw("msi_sig", s_s)
            sc.emit("scalar", lambda e: e.activation(rsg[:], sss[:], AF.Exp, scale=-0.5),
                    waits=R("sss") + W("rsg") + [(s_s, sc.cur(s_s))], inc=s_s)
            setw("rsg", s_s)
            addr("sss", s_s)
            p_t2 = PSB[0:2, 512:640]
            for tt in range(BT):
                def mk(tt=tt):
                    return lambda e: e.transpose(p_t2, msi[:, 2 * tt:2 * tt + 2], sid[:])
                sc.emit("tensor", mk(), waits=BAR + R("msi_sig") + [(s_v, sc.cur(s_v))] +
                        W("PB5"), inc=s_t)
                setw("PB5", s_t)
                def mkc(tt=tt):
                    return lambda e: e.tensor_copy(statsT[0:2, tt * 128:tt * 128 + 128],
                                                   p_t2)
                sc.emit("vector", mkc(), waits=R("PB5") + (W("statsT") if tt == 0 else []),
                        inc=s_v)
                addr("PB5", s_v)
            setw("statsT", s_v)
            addr("msi_sig", s_t)

            p_tr = PSB[:, 1024:1152]
            for tt in range(BT):
                for cch in range(8):
                    def mk(tt=tt, cch=cch):
                        return lambda e: e.transpose(
                            p_tr, r2tm[:, tt * DM + cch * 128:tt * DM + cch * 128 + 128],
                            sid[:])
                    sc.emit("tensor", mk(), waits=BAR + R("r2tm") + W("PB6"), inc=s_t)
                    setw("PB6", s_t)
                    def mkc(tt=tt, cch=cch):
                        return lambda e: e.tensor_copy(
                            r2fm[:, cch * BLK + tt * 128:cch * BLK + tt * 128 + 128], p_tr)
                    sc.emit("vector", mkc(), waits=R("PB6") +
                            (W("r2fm") if (tt == 0 and cch == 0) else []), inc=s_v)
                    addr("PB6", s_v)
            setw("r2fm", s_v)
            addr("r2tm", s_t)

            p_gs = PSB[0:128, 0:8]
            for tt in range(BT):
                for cch in range(8):
                    def mk(tt=tt, cch=cch):
                        return lambda e: e.matmul(
                            p_gs, r2fm[:, cch * BLK + tt * 128:cch * BLK + tt * 128 + 128],
                            gw_sb[:, cch * 8:cch * 8 + 8],
                            start=(cch == 0), stop=False)
                    ww = (R("r2fm", "gaug0") + W("PB45")) if cch == 0 else []
                    sc.emit("tensor", mk(), waits=ww, inc=s_t)
                def mka(tt=tt):
                    return lambda e: e.matmul(p_gs, statsT[0:2, tt * 128:tt * 128 + 128],
                                              gaug[0:2, :], start=False, stop=True)
                sc.emit("tensor", mka(), waits=R("statsT"), inc=s_t)
                setw("PB45", s_t)
                def mkev(tt=tt):
                    return lambda e: e.activation(gsc[:, tt * 8:tt * 8 + 8], p_gs, AF.Copy,
                                                  scale=rsg[:, tt:tt + 1])
                sc.emit("scalar", mkev(), waits=R("PB45", "rsg") + [(s_s, sc.cur(s_s))] +
                        (W("gsc") if tt == 0 else []), inc=s_s)
                addr("PB45", s_s)
            setw("gsc", s_s)

            for tt in range(BT):
                g0 = gsc[:, tt * 8:tt * 8 + 8]
                t_s, t_p = tk[:, 0:8], tk[:, 8:16]
                t_eq, t_cd = tk[:, 16:24], tk[:, 24:32]
                t_sel, t_wm = tk[:, 32:40], tk[:, 40:48]
                c_m, c_i = tk1[:, 0:1], tk1[:, 1:2]
                c_z, c_t = tk1[:, 2:3], tk1[:, 3:4]
                sc.emit("vector", lambda e, g0=g0, t_s=t_s: e.tensor_scalar(
                        t_s, g0, GEPS, None, op0=ALU.max),
                        waits=R("gsc") + W("tk") + [(s_v, sc.cur(s_v))], inc=s_v)
                sc.emit("scalar", lambda e, t_s=t_s, t_p=t_p: e.activation(t_p, t_s, AF.Exp),
                        waits=[(s_v, sc.cur(s_v)), (s_s, sc.cur(s_s))], inc=s_s)
                sc.emit("vector", lambda e, t_p=t_p, c_z=c_z: e.tensor_reduce(
                        c_z, t_p, axis=AX.X, op=ALU.add),
                        waits=[(s_s, sc.cur(s_s)), (s_v, sc.cur(s_v))] + W("tk1"), inc=s_v)
                sc.emit("vector", lambda e, c_z=c_z: e.reciprocal(c_z, c_z),
                        waits=[(s_v, sc.cur(s_v))], inc=s_v)
                sc.emit("vector", lambda e, t_p=t_p, c_z=c_z: e.tensor_scalar(
                        t_p, t_p, c_z, None, op0=ALU.mult),
                        waits=[(s_v, sc.cur(s_v))], inc=s_v)
                sc.emit("vector", lambda e, t_p=t_p, c_m=c_m: e.tensor_reduce(
                        c_m, t_p, axis=AX.X, op=ALU.max),
                        waits=[(s_v, sc.cur(s_v))], inc=s_v)
                sc.emit("vector", lambda e, t_p=t_p, c_m=c_m, t_eq=t_eq: e.tensor_scalar(
                        t_eq, t_p, c_m, None, op0=ALU.is_equal),
                        waits=[(s_v, sc.cur(s_v))], inc=s_v)
                sc.emit("vector", lambda e, t_cd=t_cd, t_eq=t_eq: e.tensor_mul(
                        t_cd, t_eq, sio99[:]),
                        waits=BAR + [(s_v, sc.cur(s_v))], inc=s_v)
                sc.emit("vector", lambda e, t_cd=t_cd: e.tensor_scalar(
                        t_cd, t_cd, 99.0, None, op0=ALU.add),
                        waits=[(s_v, sc.cur(s_v))], inc=s_v)
                sc.emit("vector", lambda e, t_cd=t_cd, c_i=c_i: e.tensor_reduce(
                        c_i, t_cd, axis=AX.X, op=ALU.min),
                        waits=[(s_v, sc.cur(s_v))], inc=s_v)
                sc.emit("vector", lambda e, c_i=c_i, t_sel=t_sel: e.tensor_scalar(
                        t_sel, siota[:], c_i, None, op0=ALU.is_equal),
                        waits=[(s_v, sc.cur(s_v))], inc=s_v)
                sc.emit("vector", lambda e, t_sel=t_sel, t_p=t_p, t_eq=t_eq:
                        e.scalar_tensor_tensor(t_eq, t_sel, -1e30, t_p,
                                               op0=ALU.mult, op1=ALU.add),
                        waits=[(s_v, sc.cur(s_v))], inc=s_v)
                sc.emit("vector", lambda e, t_eq=t_eq, c_m=c_m: e.tensor_reduce(
                        c_m, t_eq, axis=AX.X, op=ALU.max),
                        waits=[(s_v, sc.cur(s_v))], inc=s_v)
                sc.emit("vector", lambda e, t_eq=t_eq, c_m=c_m: e.tensor_scalar(
                        t_eq, t_eq, c_m, None, op0=ALU.is_equal),
                        waits=[(s_v, sc.cur(s_v))], inc=s_v)
                sc.emit("vector", lambda e, t_cd=t_cd, t_eq=t_eq: e.tensor_mul(
                        t_cd, t_eq, sio99[:]),
                        waits=[(s_v, sc.cur(s_v))], inc=s_v)
                sc.emit("vector", lambda e, t_cd=t_cd: e.tensor_scalar(
                        t_cd, t_cd, 99.0, None, op0=ALU.add),
                        waits=[(s_v, sc.cur(s_v))], inc=s_v)
                sc.emit("vector", lambda e, t_cd=t_cd, c_i=c_i: e.tensor_reduce(
                        c_i, t_cd, axis=AX.X, op=ALU.min),
                        waits=[(s_v, sc.cur(s_v))], inc=s_v)
                sc.emit("vector", lambda e, c_i=c_i, t_eq=t_eq: e.tensor_scalar(
                        t_eq, siota[:], c_i, None, op0=ALU.is_equal),
                        waits=[(s_v, sc.cur(s_v))], inc=s_v)
                sc.emit("vector", lambda e, t_sel=t_sel, t_eq=t_eq: e.tensor_add(
                        t_sel, t_sel, t_eq),
                        waits=[(s_v, sc.cur(s_v))], inc=s_v)
                sc.emit("vector", lambda e, t_p=t_p, t_wm=t_wm: e.tensor_scalar(
                        t_wm, t_p, WTHR, None, op0=ALU.max),
                        waits=[(s_v, sc.cur(s_v))], inc=s_v)
                sc.emit("vector", lambda e, t_wm=t_wm, t_sel=t_sel: e.tensor_mul(
                        t_wm, t_wm, t_sel),
                        waits=[(s_v, sc.cur(s_v))], inc=s_v)
                sc.emit("vector", lambda e, t_wm=t_wm, c_t=c_t: e.tensor_reduce(
                        c_t, t_wm, axis=AX.X, op=ALU.add),
                        waits=[(s_v, sc.cur(s_v))], inc=s_v)
                sc.emit("vector", lambda e, c_t=c_t: e.tensor_scalar(
                        c_t, c_t, 2.0, None, op0=ALU.mult),
                        waits=[(s_v, sc.cur(s_v))], inc=s_v)
                sc.emit("vector", lambda e, c_t=c_t: e.reciprocal(c_t, c_t),
                        waits=[(s_v, sc.cur(s_v))], inc=s_v)
                sc.emit("vector", lambda e, c_t=c_t, tt=tt: e.tensor_mul(
                        c_t, c_t, rsg[:, tt:tt + 1]),
                        waits=[(s_v, sc.cur(s_v))] + R("rsg"), inc=s_v)
                sc.emit("vector", lambda e, t_wm=t_wm, c_t=c_t, tt=tt: e.tensor_scalar(
                        comb[:, tt * 8:tt * 8 + 8], t_wm, c_t, None, op0=ALU.mult),
                        waits=[(s_v, sc.cur(s_v))] + (W("comb") if tt == 0 else []),
                        inc=s_v)
            setw("comb", s_v)
            addr("gsc", s_v)
            addr("rsg", s_v)

            sc.emit("vector", lambda e: e.memset(accb[:], 0.0),
                    waits=W("accb") + [(s_v, sc.cur(s_v))], inc=s_v)
            setw("accb", s_v)
            for e_ in range(8):
                buf = nstream % 2 if False else (b * 8 + e_) % 2
                webuf = web[buf]
                wname = f"web{buf}"
                if b == 0:
                    for half in range(2):
                        p_sw = PSA[0:1, 1024 + half * 512:1024 + half * 512 + 512]
                        for cch in range(8):
                            def mk(cch=cch, half=half, webuf=webuf):
                                return lambda e: e.matmul(
                                    p_sw if False else PSA[0:1, 1024 + half * 512:1536 + half * 512],
                                    sones[:],
                                    webuf[:, cch * DM + half * 512:cch * DM + half * 512 + 512],
                                    start=(cch == 0), stop=(cch == 7))
                            ww = (R(wname) + W("PSA23")) if (cch == 0 and half == 0) else []
                            sc.emit("tensor", mk(), waits=ww, inc=s_t)
                    setw("PSA23", s_t)
                    def mksw(e_=e_):
                        return lambda e: e.tensor_scalar(
                            statsW[0:1, e_ * DM:e_ * DM + DM], PSA[0:1, 1024:2048],
                            -float(DM), None, op0=ALU.mult)
                    sc.emit("vector", mksw(), waits=R("PSA23") +
                            (W("statsW0") if e_ == 0 else []), inc=s_v)
                    addr("PSA23", s_v)
                    wr[f"sW{e_}"] = [(s_v, sc.cur(s_v))]
                    if e_ == 7:
                        setw("statsW0", s_v)
                if nstream < NB * 8:
                    nxt = nstream % 8
                    we_load(nxt, (nstream) % 2 if False else nstream % 2)
                    nstream += 1
                for tt in range(BT):
                    for half in range(2):
                        p_y = PSA[:, half * 512:half * 512 + 512]
                        for cch in range(8):
                            def mk(tt=tt, half=half, cch=cch, webuf=webuf, p_y=p_y):
                                return lambda e: e.matmul(
                                    p_y,
                                    r2fm[:, cch * BLK + tt * 128:cch * BLK + tt * 128 + 128],
                                    webuf[:, cch * DM + half * 512:cch * DM + half * 512 + 512],
                                    start=(cch == 0), stop=False)
                            ww = (R("r2fm", wname, f"sW{e_}") + W("PSA01")) if (cch == 0 and half == 0) else []
                            sc.emit("tensor", mk(), waits=ww, inc=s_t)
                        def mka(tt=tt, half=half, e_=e_, p_y=p_y):
                            return lambda e: e.matmul(
                                p_y, statsT[0:2, tt * 128:tt * 128 + 128],
                                statsW[0:2, e_ * DM + half * 512:e_ * DM + half * 512 + 512],
                                start=False, stop=True)
                        sc.emit("tensor", mka(), waits=R("statsT") if (tt == 0 and half == 0) else [],
                                inc=s_t)
                    setw("PSA01", s_t)
                    def mkcmb(tt=tt, e_=e_):
                        return lambda e: e.scalar_tensor_tensor(
                            accb[:, tt * DM:tt * DM + DM], PSA[:, 0:1024],
                            comb[:, tt * 8 + e_:tt * 8 + e_ + 1],
                            accb[:, tt * DM:tt * DM + DM],
                            op0=ALU.mult, op1=ALU.add)
                    sc.emit("vector", mkcmb(), waits=R("PSA01", "comb", "accb"), inc=s_v)
                    addr("PSA01", s_v)
                    wr["accb"] = [(s_v, sc.cur(s_v))]
                addr(wname, s_t)
            addr("r2fm", s_t)
            addr("statsT", s_t)
            addr("comb", s_v)

            for tt in range(BT):
                def mkf(tt=tt):
                    return lambda e: e.tensor_add(accb[:, tt * DM:tt * DM + DM],
                                                  accb[:, tt * DM:tt * DM + DM],
                                                  r2tm[:, tt * DM:tt * DM + DM])
                sc.emit("vector", mkf(), waits=R("accb", "r2tm") + [(s_v, sc.cur(s_v))],
                        inc=s_v)
                wr["accb"] = [(s_v, sc.cur(s_v))]
                def mko(tt=tt, t0=t0):
                    return lambda e: e.dma_start(
                        out=yout[t0 + tt * 128:t0 + tt * 128 + 128, :],
                        in_=accb[:, tt * DM:tt * DM + DM])
                sc.emit("gpsimd", mko(), waits=[(s_o, sc.cur(s_o))] + R("accb"), inc=s_o)
                addr("accb", s_o)
            addr("r2tm", s_v)

        sc.emit("gpsimd", lambda e: e.nop(), waits=[(s_o, sc.cur(s_o))])

        block = E(nc.Block())
        sc.finalize(block)
    return nc


def ref_tail_np(afm_b, x_b, out_w, out_b, gate_W, gate_b, expert_W, expert_b):
    a = afm_b.astype(np.float64).T
    x = x_b.astype(np.float64)
    r2 = x + a @ out_w.astype(np.float64).T + out_b.astype(np.float64)
    mu = r2.mean(1, keepdims=True)
    sig = np.sqrt(r2.var(1) + EPS_LN)[:, None]
    ln = (r2 - mu) / sig
    scores = np.maximum(ln @ gate_W.astype(np.float64) + gate_b.astype(np.float64), GEPS)
    e = np.exp(scores - scores.max(1, keepdims=True))
    p = e / e.sum(1, keepdims=True)
    T = p.shape[0]
    comb = np.zeros_like(p)
    for t in range(T):
        idx = np.argsort(-p[t], kind="stable")[:2]
        w_ = np.maximum(p[t], WTHR)
        tot = w_[idx].sum()
        comb[t, idx] = w_[idx] / tot
    y = np.einsum("td,edf->tef", ln, expert_W.astype(np.float64)) + expert_b.astype(np.float64)
    out = np.einsum("tef,te->tf", y, comb) / 2.0
    return r2 + out

def ref_attn_np(x, wqk_slices, wv_slices, bqk_s, bv_s):
    xs = x.astype(np.float64)
    mu = xs.mean(1, keepdims=True)
    var = xs.var(1)
    sig = np.sqrt(var + EPS_LN)
    ln = (xs - mu) / sig[:, None]
    outs = []
    for wqk_h, wv_h, bqk_h, bv_h in zip(wqk_slices, wv_slices, bqk_s, bv_s):
        qk = ln @ wqk_h.astype(np.float64) + bqk_h.astype(np.float64)
        q, k = qk[:, 0:64], qk[:, 64:128]
        v = ln @ wv_h.astype(np.float64) + bv_h.astype(np.float64)
        qn = q / np.maximum(np.linalg.norm(q, axis=1, keepdims=True), 1e-12)
        kn = k / np.maximum(np.linalg.norm(k, axis=1, keepdims=True), 1e-12)
        c = np.clip(qn @ kn.T, -1 + 1e-7, 1 - 1e-7)
        g = np.arccos(c)
        s = (1.0 + g) ** POW
        col = s.sum(0) ** -0.5
        num = (s * col[None, :]) @ v
        den = (s * col[None, :]).sum(1, keepdims=True)
        outs.append((num / den).T)
    return outs



def _attn_inputs_for_core(core, x, qkv_w, qkv_b, S=S_FULL, hpc=HPC):
    b = core // (NCORES // B)
    h0 = (core % (NCORES // B)) * hpc
    xfm = np.ascontiguousarray(x[b].T)
    wqk = np.empty((hpc, DM, 128), np.float32)
    wv = np.empty((hpc, DM, 64), np.float32)
    bqk = np.empty((hpc, 1, 128), np.float32)
    bv = np.empty((hpc, 1, 64), np.float32)
    for i in range(hpc):
        h = h0 + i
        wq = qkv_w[h * 64:(h + 1) * 64, :]
        wk = qkv_w[DM + h * 64:DM + (h + 1) * 64, :]
        wvh = qkv_w[2 * DM + h * 64:2 * DM + (h + 1) * 64, :]
        wqk[i, :, 0:64] = wq.T
        wqk[i, :, 64:128] = wk.T
        wv[i] = wvh.T
        bqk[i, 0, 0:64] = qkv_b[h * 64:(h + 1) * 64]
        bqk[i, 0, 64:128] = qkv_b[DM + h * 64:DM + (h + 1) * 64]
        bv[i, 0] = qkv_b[2 * DM + h * 64:2 * DM + (h + 1) * 64]
    c_od = np.full((128, 1), 1.0 / DM, np.float32)
    c_sel = np.zeros((128, 2), np.float32)
    c_sel[0:64, 0] = 1.0
    c_sel[64:128, 1] = 1.0
    c_bc2 = np.zeros((2, 128), np.float32)
    c_bc2[0, 0:64] = 1.0
    c_bc2[1, 64:128] = 1.0
    c_o64 = np.ones((1, 64), np.float32)
    c_id = np.eye(128, dtype=np.float32)
    return {"xfm": xfm, "wqk": wqk, "wv": wv, "bqk": bqk, "bv": bv,
            "c_od": c_od, "c_sel": c_sel, "c_bc2": c_bc2, "c_o64": c_o64,
            "c_id": c_id, "cst": _consts_np()}


def _tail_inputs_for_core(core, afm, x, out_w, out_b, gate_W, gate_b,
                          expert_W, expert_b, TOK=S_FULL * B // NCORES):
    b = core // (NCORES // B)
    t0 = (core % (NCORES // B)) * TOK
    return {
        "afm": np.ascontiguousarray(afm[b][:, t0:t0 + TOK]),
        "xtm": np.ascontiguousarray(x[b][t0:t0 + TOK, :]),
        "wo": np.ascontiguousarray(out_w.T),
        "ob": out_b[None, :].astype(np.float32),
        "gw": gate_W,
        "gb": gate_b[None, :].astype(np.float32),
        "ew": expert_W,
        "eb": np.ascontiguousarray(expert_b[:, None, :]),
        "c_od": np.full((128, 1), 1.0 / DM, np.float32),
        "c_o128": np.ones((1, 128), np.float32),
        "c_id": np.eye(128, dtype=np.float32),
        "c_iota": np.broadcast_to(np.arange(8, dtype=np.float32), (128, 8)).copy(),
        "c_io99": np.broadcast_to(np.arange(8, dtype=np.float32) - 99.0, (128, 8)).copy(),
        "cst": _consts_np(),
    }


def kernel(L0_attn_x, L0_attn_qkv_w, L0_attn_qkv_b, L0_attn_out_w, L0_attn_out_b,
           L0_ffn_gate_W, L0_ffn_gate_b, L0_ffn_expert_W, L0_ffn_expert_b):
    x = np.ascontiguousarray(np.asarray(L0_attn_x, np.float32))
    qkv_w = np.ascontiguousarray(np.asarray(L0_attn_qkv_w, np.float32))
    qkv_b = np.ascontiguousarray(np.asarray(L0_attn_qkv_b, np.float32))
    out_w = np.ascontiguousarray(np.asarray(L0_attn_out_w, np.float32))
    out_b = np.ascontiguousarray(np.asarray(L0_attn_out_b, np.float32))
    gate_W = np.ascontiguousarray(np.asarray(L0_ffn_gate_W, np.float32))
    gate_b = np.ascontiguousarray(np.asarray(L0_ffn_gate_b, np.float32))
    expert_W = np.ascontiguousarray(np.asarray(L0_ffn_expert_W, np.float32))
    expert_b = np.ascontiguousarray(np.asarray(L0_ffn_expert_b, np.float32))

    nc1 = build_attn()
    in_maps = [_attn_inputs_for_core(c, x, qkv_w, qkv_b) for c in range(NCORES)]
    res1 = run_bass_kernel_spmd(nc1, in_maps, core_ids=list(range(NCORES)))
    afm = np.empty((B, DM, S_FULL), np.float32)
    for c in range(NCORES):
        b = c // (NCORES // B)
        h0 = (c % (NCORES // B)) * HPC
        a = res1.results[c]["attn"]
        for i in range(HPC):
            afm[b, (h0 + i) * 64:(h0 + i + 1) * 64, :] = a[i]

    nc2 = build_tail()
    in_maps2 = [_tail_inputs_for_core(c, afm, x, out_w, out_b, gate_W, gate_b,
                                      expert_W, expert_b) for c in range(NCORES)]
    res2 = run_bass_kernel_spmd(nc2, in_maps2, core_ids=list(range(NCORES)))
    TOK = S_FULL * B // NCORES
    out = np.empty((B, S_FULL, DM), np.float32)
    for c in range(NCORES):
        b = c // (NCORES // B)
        t0 = (c % (NCORES // B)) * TOK
        out[b, t0:t0 + TOK, :] = res2.results[c]["yout"]
    return out


# version 05
# speedup vs baseline: 1.4910x; 1.4910x over previous
import math
from contextlib import ExitStack

import numpy as np

import concourse.bass as bass
import concourse.mybir as mybir
from concourse.bass_utils import run_bass_kernel_spmd

F32 = mybir.dt.float32
AF = mybir.ActivationFunctionType
ALU = mybir.AluOpType
AX = mybir.AxisListType

B, S_FULL, DM = 2, 8192, 1024
H, HD = 16, 64
NCORES = 8
HPC = H * B // NCORES
EPS_LN = 1e-5
PI2P1 = 1.0 + math.pi / 2.0
POW = -65.0
GEPS, WTHR = 1e-4, 0.1

CSCL = float(2.0 ** 69)
CONST_VALS = [PI2P1, POW, EPS_LN, -0.5, 0.5, 2.0, -2.0, 1.0 / DM, -float(DM),
              GEPS, WTHR, 99.0, -1e30, 0.25, CSCL, -4.0]


class Sched:

    ENGS = ("sync", "scalar", "vector", "tensor", "gpsimd")

    def __init__(self, nc, ctx):
        self.nc = nc
        self.ctx = ctx
        self.ops = []
        self.sems = {}
        self.step = {}
        self.count = {}

    def sem(self, name, step=1):
        if name not in self.sems:
            self.sems[name] = self.ctx.enter_context(self.nc.semaphore(name))
            self.step[name] = step
            self.count[name] = 0
        return name

    def cur(self, name):
        return self.count[name]

    def emit(self, eng, fn, waits=(), inc=None):
        wl = []
        for w in waits:
            if isinstance(w, tuple):
                wl.append(w)
            else:
                wl.append((w, self.count[w]))
        self.ops.append((eng, fn, wl, inc))
        if inc is not None:
            self.count[inc] += self.step[inc]
        return self.count[inc] if inc is not None else None

    def finalize(self, block):
        per = {e: [] for e in self.ENGS}
        for op in self.ops:
            per[op[0]].append(op)
        sched = self

        def make(engname, oplist):
            def body(eng):
                last = {}
                for _, fn, wl, inc in oplist:
                    for name, val in wl:
                        if val > 0 and last.get(name, -1) < val:
                            eng.wait_ge(sched.sems[name], val)
                            last[name] = val
                    instr = fn(eng)
                    if inc is not None:
                        instr.then_inc(sched.sems[inc], sched.step[inc])
                return None
            return body

        for engname in self.ENGS:
            if not per[engname]:
                continue
            getattr(block, engname)(make(engname, per[engname]))


def _consts_np():
    c = np.zeros((128, len(CONST_VALS)), np.float32)
    for j, v in enumerate(CONST_VALS):
        c[:, j] = v
    return c


def _register_consts(nc, cst_ap):
    for j, v in enumerate(CONST_VALS):
        nc.const_aps.register(F32, float(v), cst_ap[:, j:j + 1])




def build_attn(S=S_FULL, hpc=HPC, debug=False):
    NW = S // 512
    NR = S // 128
    GW = min(1024, S)
    NG = S // GW
    NC_ = S // 128

    nc = bass.Bass("TRN2", target_bir_lowering=False, debug=False)

    xfm = nc.dram_tensor("xfm", [DM, S], F32, kind="ExternalInput").ap()
    wqk = nc.dram_tensor("wqk", [hpc, DM, 128], F32, kind="ExternalInput").ap()
    wv = nc.dram_tensor("wv", [hpc, DM, 64], F32, kind="ExternalInput").ap()
    bqk = nc.dram_tensor("bqk", [hpc, 1, 128], F32, kind="ExternalInput").ap()
    bv = nc.dram_tensor("bv", [hpc, 1, 64], F32, kind="ExternalInput").ap()
    c_od = nc.dram_tensor("c_od", [128, 1], F32, kind="ExternalInput").ap()
    c_sel = nc.dram_tensor("c_sel", [128, 2], F32, kind="ExternalInput").ap()
    c_bc2 = nc.dram_tensor("c_bc2", [2, 128], F32, kind="ExternalInput").ap()
    c_o64 = nc.dram_tensor("c_o64", [1, 64], F32, kind="ExternalInput").ap()
    c_id = nc.dram_tensor("c_id", [128, 128], F32, kind="ExternalInput").ap()
    cst = nc.dram_tensor("cst", [128, len(CONST_VALS)], F32, kind="ExternalInput").ap()
    attn = nc.dram_tensor("attn", [hpc, 64, S], F32, kind="ExternalOutput").ap()
    if debug:
        dkst = nc.dram_tensor("dkst", [128, S], F32, kind="ExternalOutput").ap()
        dqk = nc.dram_tensor("dqk", [128, S], F32, kind="ExternalOutput").ap()
        dacc = nc.dram_tensor("dacc", [65, S], F32, kind="ExternalOutput").ap()
        dfrow = nc.dram_tensor("dfrow", [128, S], F32, kind="ExternalOutput").ap()
        dcs = nc.dram_tensor("dcs", [128, S // 128], F32, kind="ExternalOutput").ap()
        dvav = nc.dram_tensor("dvav", [128, S // 128], F32, kind="ExternalOutput").ap()

    with ExitStack() as ctx:
        E = ctx.enter_context
        AR = max(S, 8192)
        frow = E(nc.sbuf_tensor("frow", [128, AR], F32))
        zb = [E(nc.sbuf_tensor(f"zb{i}", [128, GW], F32)) for i in range(2)]
        cb = [E(nc.sbuf_tensor(f"cb{i}", [128, GW], F32)) for i in range(2)]
        qkhat = E(nc.sbuf_tensor("qkhat", [128, S], F32))
        kst = E(nc.sbuf_tensor("kst", [128, S], F32))
        khat = kst.tensor if False else kst
        xrow = E(nc.sbuf_tensor("xrow", [128, 1024], F32))
        acc = E(nc.sbuf_tensor("acc", [65, S], F32))
        augt = E(nc.sbuf_tensor("augt", [66, 192], F32))
        mstm = E(nc.sbuf_tensor("mstm", [128, 2 * NC_], F32))
        sstm = E(nc.sbuf_tensor("sstm", [128, NC_], F32))
        vtmp = E(nc.sbuf_tensor("vtmp", [128, NC_], F32))
        rstm = E(nc.sbuf_tensor("rstm", [128, NC_], F32))
        wsb = E(nc.sbuf_tensor("wsb", [128, 8 * 128], F32))
        wvsb = E(nc.sbuf_tensor("wvsb", [128, 8 * 64], F32))
        qksq = E(nc.sbuf_tensor("qksq", [128, 512], F32))
        qkc = E(nc.sbuf_tensor("qkc", [128, 512], F32))
        ln2b = E(nc.sbuf_tensor("ln2b", [2, 512], F32))
        scl2 = E(nc.sbuf_tensor("scl2", [2, 512], F32))
        vch = [E(nc.sbuf_tensor(f"vch{i}", [128, 64], F32)) for i in range(2)]
        vav = [E(nc.sbuf_tensor(f"vav{i}", [128, 65], F32)) for i in range(2)]
        colsum = E(nc.sbuf_tensor("colsum", [128, 1], F32))
        cj = E(nc.sbuf_tensor("cj", [128, 1], F32))
        sones = E(nc.sbuf_tensor("sones", [128, 1], F32))
        ssel = E(nc.sbuf_tensor("ssel", [128, 2], F32))
        sbc2 = E(nc.sbuf_tensor("sbc2", [2, 128], F32))
        so64t = E(nc.sbuf_tensor("so64t", [65, 64], F32))
        sid = E(nc.sbuf_tensor("sid", [128, 128], F32))
        scst = E(nc.sbuf_tensor("scst", [128, len(CONST_VALS)], F32))
        dstg = E(nc.sbuf_tensor("dstg", [64, 512], F32))
        dstg2 = E(nc.sbuf_tensor("dstg2", [64, 512], F32))
        if debug:
            dcsb = E(nc.sbuf_tensor("dcsb", [128, S // 128], F32))
            dvavb = E(nc.sbuf_tensor("dvavb", [128, S // 128], F32))
        PSA = E(nc.psum_tensor("PSAg0", [128, 2048], F32))
        PSB = E(nc.psum_tensor("PSB", [128, 2048], F32))

        _register_consts(nc, scst[:])

        sc = Sched(nc, ctx)
        s_ld = sc.sem("s_ld", 16)
        s_x = sc.sem("s_x", 16)
        s_t = sc.sem("s_t")
        s_v = sc.sem("s_v")
        s_s = sc.sem("s_s")
        s_g = sc.sem("s_g")
        s_kd = sc.sem("s_kd", 16)
        s_o = sc.sem("s_o", 16)
        s_xr = sc.sem("s_xr", 16)

        wr = {}
        rd = {}

        def W(buf):
            return list(rd.get(buf, [])) + list(wr.get(buf, []))

        def R(*bufs):
            out = []
            for b in bufs:
                out += list(wr.get(b, []))
            return out

        def setw(buf, sem):
            wr[buf] = [(sem, sc.cur(sem))]
            rd[buf] = []

        def addr(buf, sem):
            rd.setdefault(buf, [])
            rd[buf] = [x for x in rd[buf] if x[0] != sem] + [(sem, sc.cur(sem))]

        def fold(dst, *srcs):
            rd.setdefault(dst, [])
            for b in srcs:
                rd[dst] = rd[dst] + rd.pop(b, []) + wr.pop(b, [])

        for t, s, b in ((sones[:, 0:1], c_od[:], "sones"), (ssel[:], c_sel[:], "ssel"),
                        (sbc2[:], c_bc2[:], "sbc2"), (so64t[64:65, :], c_o64[:], "so64t"),
                        (sid[:], c_id[:], "sid"), (scst[:], cst[:], "scst")):
            def mk(t=t, s=s):
                return lambda e: e.dma_start(out=t, in_=s)
            sc.emit("sync", mk(), inc=s_ld)
        BAR = [(s_ld, sc.cur(s_ld))]

        xw = lambda par, c: frow[:, (par * 8 + c) * 512:(par * 8 + c) * 512 + 512]

        p_mu = PSB[0:1, 0:512]
        p_ss = PSB[0:1, 512:1024]
        p_t1 = PSB[:, 1024:1025]
        p_t2 = PSB[0:2, 1536:1664]
        for w in range(NW):
            par = w % 2
            for cch in range(8):
                def mk(w=w, cch=cch, par=par):
                    return lambda e: e.dma_start(
                        out=xw(par, cch),
                        in_=xfm[cch * 128:cch * 128 + 128, w * 512:w * 512 + 512])
                w0 = ([(s_x, sc.cur(s_x))] if cch == 0 else []) + (W(f"xw{par}") if cch == 0 else [])
                sc.emit("sync", mk(), waits=w0, inc=s_x)
            xv = sc.cur(s_x)
            setw(f"xw{par}", s_x)
            for cch in range(8):
                def mk(cch=cch, par=par):
                    return lambda e: e.matmul(p_mu, sones[:], xw(par, cch),
                                              start=(cch == 0), stop=(cch == 7))
                sc.emit("tensor", mk(), waits=BAR + [(s_x, xv)] + W("PB45"), inc=s_t)
            setw("PB45", s_t)
            for cch in range(8):
                def mkv(cch=cch, par=par):
                    return lambda e: e.tensor_mul(qksq[:], xw(par, cch), xw(par, cch))
                sc.emit("vector", mkv(), waits=[(s_x, xv)] + W("qksq"), inc=s_v)
                setw("qksq", s_v)
                def mkt(cch=cch):
                    return lambda e: e.matmul(p_ss, sones[:], qksq[:],
                                              start=(cch == 0), stop=(cch == 7))
                sc.emit("tensor", mkt(), waits=BAR + R("qksq") + (W("PB45") if cch == 0 else []),
                        inc=s_t)
                addr("qksq", s_t)
            setw("PB45", s_t)
            addr(f"xw{par}", s_t)
            addr(f"xw{par}", s_v)
            sc.emit("vector", lambda e: e.tensor_copy(qkc[0:1, 0:512], p_mu),
                    waits=R("PB45") + W("qkc"), inc=s_v)
            setw("qkc", s_v)
            addr("PB45", s_v)
            sc.emit("vector", lambda e: e.tensor_copy(qksq[0:1, 0:512], p_ss),
                    waits=R("PB45") + W("qksq"), inc=s_v)
            setw("qksq", s_v)
            addr("PB45", s_v)
            for t4 in range(4):
                cidx = w * 4 + t4
                def mk1(t4=t4):
                    return lambda e: e.transpose(p_t1, qkc[0:1, t4 * 128:t4 * 128 + 128],
                                                 sid[0:1, 0:1])
                sc.emit("tensor", mk1(), waits=BAR + R("qkc") + W("PB6"), inc=s_t)
                setw("PB6", s_t)
                addr("qkc", s_t)
                def mkc1(cidx=cidx):
                    return lambda e: e.tensor_copy(mstm[:, 2 * cidx:2 * cidx + 1], p_t1)
                sc.emit("vector", mkc1(), waits=R("PB6"), inc=s_v)
                addr("PB6", s_v)
                def mk2(t4=t4):
                    return lambda e: e.transpose(p_t1, qksq[0:1, t4 * 128:t4 * 128 + 128],
                                                 sid[0:1, 0:1])
                sc.emit("tensor", mk2(), waits=BAR + R("qksq") + W("PB6"), inc=s_t)
                setw("PB6", s_t)
                addr("qksq", s_t)
                def mkc2(cidx=cidx):
                    return lambda e: e.tensor_copy(sstm[:, cidx:cidx + 1], p_t1)
                sc.emit("vector", mkc2(), waits=R("PB6"), inc=s_v)
                addr("PB6", s_v)
        setw("mstm_mu", s_v)
        setw("sstm", s_v)
        mu_tm = mstm[:].rearrange("p (c t) -> p c t", t=2)
        sc.emit("vector", lambda e: e.tensor_mul(vtmp[:], mu_tm[:, :, 0], mu_tm[:, :, 0]),
                waits=R("mstm_mu"), inc=s_v)
        setw("vtmp", s_v)
        sc.emit("vector", lambda e: e.tensor_sub(sstm[:], sstm[:], vtmp[:]),
                waits=R("sstm", "vtmp"), inc=s_v)
        setw("sstm", s_v)
        sc.emit("scalar", lambda e: e.activation(sstm[:], sstm[:], AF.Ln,
                                                 bias=EPS_LN, scale=1.0),
                waits=BAR + R("sstm"), inc=s_s)
        setw("sstm", s_s)
        sc.emit("scalar", lambda e: e.activation(mu_tm[:, :, 1], sstm[:], AF.Exp, scale=0.5),
                waits=R("sstm"), inc=s_s)
        setw("mstm_sig", s_s)
        sc.emit("scalar", lambda e: e.activation(rstm[:], sstm[:], AF.Exp, scale=-0.5),
                waits=R("sstm"), inc=s_s)
        setw("rstm", s_s)
        addr("sstm", s_s)
        for c in range(NC_):
            def mk(c=c):
                return lambda e: e.transpose(p_t2, mstm[:, 2 * c:2 * c + 2], sid[:])
            sc.emit("tensor", mk(), waits=BAR + R("mstm_mu", "mstm_sig") + W("PB7"),
                    inc=s_t)
            setw("PB7", s_t)
            def mkc(c=c):
                return lambda e: e.tensor_copy(acc[0:2, c * 128:c * 128 + 128], p_t2)
            sc.emit("vector", mkc(), waits=R("PB7"), inc=s_v)
            addr("PB7", s_v)
        setw("acc02", s_v)
        addr("mstm_mu", s_t)
        addr("mstm_sig", s_t)
        sc.emit("gpsimd", lambda e: e.dma_start(out=kst[64:66, 0:S], in_=acc[0:2, 0:S]),
                waits=R("acc02") + [(s_kd, sc.cur(s_kd))], inc=s_kd)
        wr["stats"] = [(s_kd, sc.cur(s_kd))]
        rd["stats"] = []
        addr("acc02", s_kd)
        fold("acc", "acc02")

        for p in range(hpc):
            for cch in range(8):
                def mk(p=p, cch=cch):
                    return lambda e: e.dma_start(out=wsb[:, cch * 128:cch * 128 + 128],
                                                 in_=wqk[p, cch * 128:cch * 128 + 128, :])
                w0 = ([(s_ld, sc.cur(s_ld))] + W("wsb") + W("wvsb")) if cch == 0 else []
                sc.emit("sync", mk(), waits=w0, inc=s_ld)
            setw("wsb", s_ld)
            for cch in range(8):
                def mk(p=p, cch=cch):
                    return lambda e: e.dma_start(out=wvsb[:, cch * 64:cch * 64 + 64],
                                                 in_=wv[p, cch * 128:cch * 128 + 128, :])
                sc.emit("sync", mk(), inc=s_ld)
            setw("wvsb", s_ld)
            def mkb(p=p):
                return lambda e: e.dma_start(out=augt[65:66, 0:128], in_=bqk[p])
            sc.emit("sync", mkb(), waits=W("augw"), inc=s_ld)
            def mkb2(p=p):
                return lambda e: e.dma_start(out=augt[65:66, 128:192], in_=bv[p])
            sc.emit("sync", mkb2(), waits=W("augv"), inc=s_ld)
            wld = sc.cur(s_ld)
            wr["augw"] = [(s_ld, wld)]
            wr["augv"] = [(s_ld, wld)]
            rd["augw"] = []
            rd["augv"] = []
            wr["wsb"] = [(s_ld, wld)]
            wr["wvsb"] = [(s_ld, wld)]
            p_sw = PSA[0:1, 0:128]
            for cch in range(8):
                def mk(cch=cch):
                    return lambda e: e.matmul(p_sw, sones[:],
                                              wsb[:, cch * 128:cch * 128 + 128],
                                              start=(cch == 0), stop=(cch == 7))
                sc.emit("tensor", mk(), waits=BAR + R("wsb") + (W("PSAg0") if cch == 0 else []),
                        inc=s_t)
            setw("PSAg0", s_t)
            sc.emit("vector", lambda e: e.tensor_scalar(acc[0:1, 0:128], p_sw, -float(DM),
                                                        None, op0=ALU.mult),
                    waits=R("PSAg0", "augw") + W("augw_r0") + W("acc"), inc=s_v)
            setw("augw_r0", s_v)
            addr("PSAg0", s_v)
            p_swv = PSA[0:1, 512:576]
            for cch in range(8):
                def mk(cch=cch):
                    return lambda e: e.matmul(p_swv, sones[:],
                                              wvsb[:, cch * 64:cch * 64 + 64],
                                              start=(cch == 0), stop=(cch == 7))
                sc.emit("tensor", mk(), waits=BAR + R("wvsb") + (W("PSAg0") if cch == 0 else []),
                        inc=s_t)
            setw("PSAg0", s_t)
            sc.emit("vector", lambda e: e.tensor_scalar(acc[0:1, 128:192], p_swv, -float(DM),
                                                        None, op0=ALU.mult),
                    waits=R("PSAg0", "augv") + W("augv_r0") + W("acc"), inc=s_v)
            setw("augv_r0", s_v)
            addr("PSAg0", s_v)
            sc.emit("gpsimd", lambda e: e.dma_start(out=augt[64:65, 0:192],
                                                    in_=acc[0:1, 0:192]),
                    waits=[(s_kd, sc.cur(s_kd))] + R("augw_r0", "augv_r0") +
                          list(rd.get("augw", [])) + list(rd.get("augv", [])),
                    inc=s_kd)
            wr["augw_r0"] = [(s_kd, sc.cur(s_kd))]
            wr["augv_r0"] = [(s_kd, sc.cur(s_kd))]
            addr("acc", s_kd)
            aug_ready = R("augw_r0", "augv_r0", "augw", "augv")

            fold("xw0", "frow")
            rd["xw1"] = rd.get("xw1", []) + list(rd.get("xw0", []))
            p_qk = PSB[:, 1536:2048]
            p_ss2 = PSB[0:2, 1024:1536]
            p_bc = PSB[:, 0:512]
            for w in range(NW):
                par = w % 2
                for cch in range(8):
                    def mk(w=w, cch=cch, par=par):
                        return lambda e: e.dma_start(
                            out=xw(par, cch),
                            in_=xfm[cch * 128:cch * 128 + 128, w * 512:w * 512 + 512])
                    w0 = ([(s_x, sc.cur(s_x))] + W(f"xw{par}")) if cch == 0 else []
                    sc.emit("sync", mk(), waits=w0, inc=s_x)
                xv = sc.cur(s_x)
                setw(f"xw{par}", s_x)
                for cch in range(8):
                    def mk(cch=cch, par=par):
                        return lambda e: e.matmul(p_qk, wsb[:, cch * 128:cch * 128 + 128],
                                                  xw(par, cch), start=(cch == 0), stop=False)
                    ww = ([(s_x, xv)] + aug_ready + R("wsb") + W("PB7")) if cch == 0 else []
                    sc.emit("tensor", mk(), waits=ww, inc=s_t)
                def mkaug(w=w):
                    return lambda e: e.matmul(p_qk, augt[64:66, 0:128],
                                              kst[64:66, w * 512:w * 512 + 512],
                                              start=False, stop=True)
                sc.emit("tensor", mkaug(), waits=R("stats"), inc=s_t)
                setw("PB7", s_t)
                addr(f"xw{par}", s_t)
                sc.emit("vector", lambda e: e.tensor_copy(qkc[:], p_qk),
                        waits=R("PB7") + W("qkc"), inc=s_v)
                setw("qkc", s_v)
                sc.emit("vector", lambda e: e.tensor_mul(qksq[:], qkc[:], p_qk),
                        waits=R("PB7", "qkc") + W("qksq"), inc=s_v)
                setw("qksq", s_v)
                addr("PB7", s_v)
                sc.emit("tensor", lambda e: e.matmul(p_ss2, ssel[:], qksq[:],
                                                     start=True, stop=True),
                        waits=BAR + R("qksq") + W("PB6"), inc=s_t)
                setw("PB6", s_t)
                addr("qksq", s_t)
                sc.emit("scalar", lambda e: e.activation(ln2b[:], p_ss2, AF.Ln),
                        waits=BAR + R("PB6") + W("ln2b"), inc=s_s)
                setw("ln2b", s_s)
                addr("PB6", s_s)
                sc.emit("scalar", lambda e: e.activation(scl2[:], ln2b[:], AF.Exp,
                                                         scale=-0.5),
                        waits=R("ln2b") + W("scl2"), inc=s_s)
                setw("scl2", s_s)
                addr("ln2b", s_s)
                sc.emit("tensor", lambda e: e.matmul(p_bc, sbc2[:], scl2[:],
                                                     start=True, stop=True),
                        waits=BAR + R("scl2") + W("PB45"), inc=s_t)
                setw("PB45", s_t)
                addr("scl2", s_t)
                def mknrm(w=w):
                    return lambda e: e.tensor_mul(qkhat[:, w * 512:w * 512 + 512],
                                                  qkc[:], p_bc)
                sc.emit("vector", mknrm(), waits=R("PB45", "qkc") + W(f"qkhat{w}"), inc=s_v)
                setw(f"qkhat{w}", s_v)
                addr("PB45", s_v)
                addr("qkc", s_v)
                def mkkd(w=w):
                    return lambda e: e.dma_start(out=kst[0:64, w * 512:w * 512 + 512],
                                                 in_=qkhat[64:128, w * 512:w * 512 + 512])
                w0 = [(s_kd, sc.cur(s_kd))] if w == 0 else []
                sc.emit("gpsimd", mkkd(), waits=w0 + R(f"qkhat{w}") + W(f"khat{w}"),
                        inc=s_kd)
                rd[f"khat{w}"] = []
            kd_end = sc.cur(s_kd)
            for w2 in range(NW):
                wr[f"khat{w2}"] = [(s_kd, kd_end)]
                rd.setdefault(f"qkhat{w2}", []).append((s_kd, kd_end))
            qk_ready = [(s_kd, kd_end), (s_v, sc.cur(s_v))]

            fold("frow", "xw0", "xw1")
            sc.emit("vector", lambda e: e.memset(acc[:], 0.0), waits=W("acc"), inc=s_v)
            setw("acc", s_v)
            WPG = GW // 512
            p_av = PSB[0:65, 0:1024]

            def emit_av_wp(rav, wp):
                rp = rav % 2
                for half in range(WPG):
                    w = wp * WPG + half
                    def mk(w=w, half=half, rp=rp):
                        return lambda e: e.matmul(
                            PSB[0:65, half * 512:half * 512 + 512],
                            vav[rp][:], frow[:, w * 512:w * 512 + 512],
                            start=True, stop=True)
                    sc.emit("tensor", mk(), waits=R(f"vav{rp}", "frow_all") +
                            (W("PAV") if half == 0 else []), inc=s_t)
                setw("PAV", s_t)
                addr(f"vav{rp}", s_t)
                addr(f"fg{wp}", s_t)
                def mkacc(wp=wp):
                    return lambda e: e.tensor_add(acc[:, wp * 1024:wp * 1024 + 1024],
                                                  acc[:, wp * 1024:wp * 1024 + 1024],
                                                  PSB[0:65, 0:1024])
                sc.emit("vector", mkacc(), waits=R("PAV", "acc"), inc=s_v)
                addr("PAV", s_v)
                wr["acc"] = [(s_v, sc.cur(s_v))]

            for r in range(NR):
                rp = r % 2
                for cch in range(8):
                    def mk(r=r, cch=cch):
                        return lambda e: e.dma_start(
                            out=xrow[:, cch * 128:cch * 128 + 128],
                            in_=xfm[cch * 128:cch * 128 + 128, r * 128:r * 128 + 128])
                    w0 = ([(s_xr, sc.cur(s_xr))] + W("xrow")) if cch == 0 else []
                    sc.emit("sync", mk(), waits=w0, inc=s_xr)
                xrv = sc.cur(s_xr)
                setw("xrow", s_xr)
                p_v = PSB[:, 1024:1088]
                for cch in range(8):
                    def mk(cch=cch):
                        return lambda e: e.matmul(p_v, xrow[:, cch * 128:cch * 128 + 128],
                                                  wvsb[:, cch * 64:cch * 64 + 64],
                                                  start=(cch == 0), stop=False)
                    ww = ([(s_xr, xrv)] + R("wvsb") + W("PB6")) if cch == 0 else []
                    sc.emit("tensor", mk(), waits=ww, inc=s_t)
                def mkvaug(r=r):
                    return lambda e: e.matmul(p_v, kst[64:66, r * 128:r * 128 + 128],
                                              augt[64:66, 128:192], start=False, stop=True)
                sc.emit("tensor", mkvaug(), waits=aug_ready + R("stats"), inc=s_t)
                setw("PB6", s_t)
                addr("xrow", s_t)
                def mkvev(r=r, rp=rp):
                    return lambda e: e.activation(vch[rp][:], p_v, AF.Copy,
                                                  scale=rstm[:, r:r + 1])
                sc.emit("scalar", mkvev(), waits=R("PB6", "rstm") + W(f"vch{rp}") +
                        [(s_s, sc.cur(s_s))], inc=s_s)
                setw(f"vch{rp}", s_s)
                addr("PB6", s_s)

                for g in range(NG):
                    gpar = g % 2
                    qkp = PSA[:, gpar * GW:gpar * GW + GW]
                    for wi in range(WPG):
                        w = g * WPG + wi
                        def mk(r=r, w=w, wi=wi, qkp=qkp):
                            return lambda e: e.matmul(
                                qkp[:, wi * 512:wi * 512 + 512],
                                kst[0:64, r * 128:r * 128 + 128],
                                qkhat[0:64, w * 512:w * 512 + 512],
                                start=True, stop=True)
                        sc.emit("tensor", mk(), waits=qk_ready +
                                (W(f"PSAg{gpar}") if wi == 0 else []), inc=s_t)
                    setw(f"PSAg{gpar}", s_t)
                    if r > 0:
                        emit_av_wp(r - 1, g)
                    def mkl1(gpar=gpar, qkp=qkp):
                        return lambda e: e.activation(cb[gpar][:], qkp, AF.Ln,
                                                      bias=0.5, scale=0.5)
                    sc.emit("scalar", mkl1(), waits=R(f"PSAg{gpar}") + W(f"cb{gpar}") +
                            [(s_s, sc.cur(s_s))], inc=s_s)
                    setw(f"cb{gpar}", s_s)
                    def mkl2(gpar=gpar, qkp=qkp):
                        return lambda e: e.activation(zb[gpar][:], qkp, AF.Ln,
                                                      bias=0.5, scale=-0.5)
                    sc.emit("scalar", mkl2(), waits=R(f"PSAg{gpar}") + W(f"zb{gpar}"),
                            inc=s_s)
                    setw(f"zb{gpar}", s_s)
                    addr(f"PSAg{gpar}", s_s)
                    def mke1(gpar=gpar):
                        return lambda e: e.activation(cb[gpar][:], cb[gpar][:], AF.Exp,
                                                      scale=0.5)
                    sc.emit("scalar", mke1(), waits=R(f"cb{gpar}"), inc=s_s)
                    setw(f"cb{gpar}", s_s)
                    def mke2(gpar=gpar):
                        return lambda e: e.activation(zb[gpar][:], zb[gpar][:], AF.Exp,
                                                      scale=-0.5)
                    sc.emit("scalar", mke2(), waits=R(f"zb{gpar}"), inc=s_s)
                    setw(f"zb{gpar}", s_s)
                    def mku(g=g, gpar=gpar):
                        return lambda e: e.scalar_tensor_tensor(
                            frow[:, g * GW:g * GW + GW], cb[gpar][:], -1.0, zb[gpar][:],
                            op0=ALU.add, op1=ALU.mult)
                    sc.emit("vector", mku(), waits=R(f"zb{gpar}", f"cb{gpar}") +
                            W(f"fg{g}"), inc=s_v)
                    wr[f"fg{g}"] = [(s_v, sc.cur(s_v))]
                    rd[f"fg{g}"] = []
                    addr(f"zb{gpar}", s_v)
                    addr(f"cb{gpar}", s_v)
                stt_all = [(s_v, sc.cur(s_v))]
                sc.emit("scalar", lambda e: e.activation(frow[:, 0:S], frow[:, 0:S],
                                                         AF.Arctan),
                        waits=stt_all + [x for g2 in range(NG) for x in wr[f"fg{g2}"]] +
                              [(s_s, sc.cur(s_s))], inc=s_s)
                sc.emit("scalar", lambda e: e.activation(frow[:, 0:S], frow[:, 0:S], AF.Ln,
                                                         bias=1.0, scale=-4.0),
                        waits=[(s_s, sc.cur(s_s))], inc=s_s)
                def mkexp():
                    return lambda e: e.activation(frow[:, 0:S], frow[:, 0:S], AF.Exp,
                                                  scale=POW, accum_out=colsum[:])
                sc.emit("scalar", mkexp(), waits=[(s_s, sc.cur(s_s))] + W("colsum"),
                        inc=s_s)
                wr["frow_all"] = [(s_s, sc.cur(s_s))]
                setw("colsum", s_s)
                for g2 in range(NG):
                    addr(f"fg{g2}", s_s)
                sc.emit("scalar", lambda e: e.activation(cj[:], colsum[:], AF.Ln,
                                                         scale=CSCL),
                        waits=R("colsum") + W("cj"), inc=s_s)
                setw("cj", s_s)
                addr("colsum", s_s)
                sc.emit("scalar", lambda e: e.activation(cj[:], cj[:], AF.Exp, scale=-0.5),
                        waits=R("cj"), inc=s_s)
                setw("cj", s_s)
                def mkva(rp=rp):
                    return lambda e: e.tensor_scalar(vav[rp][:, 0:64], vch[rp][:],
                                                     cj[:, 0:1], None, op0=ALU.mult)
                sc.emit("vector", mkva(), waits=R("cj", f"vch{rp}") + W(f"vav{rp}") +
                        [(s_v, sc.cur(s_v))], inc=s_v)
                def mkvc(rp=rp):
                    return lambda e: e.tensor_copy(vav[rp][:, 64:65], cj[:])
                sc.emit("vector", mkvc(), waits=R("cj"), inc=s_v)
                setw(f"vav{rp}", s_v)
                addr("cj", s_v)
                addr(f"vch{rp}", s_v)
            for wp in range(NG):
                emit_av_wp(NR - 1, wp)

            for w in range(NW):
                p_db = PSB[0:64, 0:512]
                def mkb(w=w):
                    return lambda e: e.matmul(p_db, so64t[64:65, :],
                                              acc[64:65, w * 512:w * 512 + 512],
                                              start=True, stop=True)
                sc.emit("tensor", mkb(), waits=BAR + R("acc") + W("PB45"), inc=s_t)
                setw("PB45", s_t)
                addr("acc", s_t)
                def mkr(w=w):
                    return lambda e: e.reciprocal(dstg2[:], p_db)
                sc.emit("vector", mkr(), waits=R("PB45") + W("dstg2"), inc=s_v)
                setw("dstg2", s_v)
                addr("PB45", s_v)
                def mkm(w=w):
                    return lambda e: e.tensor_mul(dstg[:], acc[0:64, w * 512:w * 512 + 512],
                                                  dstg2[:])
                sc.emit("vector", mkm(), waits=R("dstg2", "acc") + W("dstg"), inc=s_v)
                setw("dstg", s_v)
                addr("dstg2", s_v)
                def mko(p=p, w=w):
                    return lambda e: e.dma_start(out=attn[p, :, w * 512:w * 512 + 512],
                                                 in_=dstg[:])
                sc.emit("gpsimd", mko(), waits=[(s_o, sc.cur(s_o))] + R("dstg"), inc=s_o)
                addr("dstg", s_o)
            addr("acc", s_t)
            for w2 in range(NW):
                addr(f"khat{w2}", s_t)
                addr(f"qkhat{w2}", s_t)
            for bname in ("wsb", "wvsb", "augw", "augv", "augw_r0", "augv_r0"):
                addr(bname, s_t)
            addr("wvsb", s_s)

        sc.emit("gpsimd", lambda e: e.nop(), waits=[(s_o, sc.cur(s_o))])

        block = E(nc.Block())
        sc.finalize(block)
    return nc



def build_tail(TOK=S_FULL * B // NCORES, BLK=512):
    NB = TOK // BLK
    BT = BLK // 128
    NE = 8

    nc = bass.Bass("TRN2", target_bir_lowering=False, debug=False)

    afm = nc.dram_tensor("afm", [DM, TOK], F32, kind="ExternalInput").ap()
    xtm = nc.dram_tensor("xtm", [TOK, DM], F32, kind="ExternalInput").ap()
    wo = nc.dram_tensor("wo", [DM, DM], F32, kind="ExternalInput").ap()
    ob = nc.dram_tensor("ob", [1, DM], F32, kind="ExternalInput").ap()
    gw = nc.dram_tensor("gw", [DM, 8], F32, kind="ExternalInput").ap()
    gb = nc.dram_tensor("gb", [1, 8], F32, kind="ExternalInput").ap()
    ew = nc.dram_tensor("ew", [8, DM, DM], F32, kind="ExternalInput").ap()
    eb = nc.dram_tensor("eb", [8, 1, DM], F32, kind="ExternalInput").ap()
    c_od = nc.dram_tensor("c_od", [128, 1], F32, kind="ExternalInput").ap()
    c_o128 = nc.dram_tensor("c_o128", [1, 128], F32, kind="ExternalInput").ap()
    c_id = nc.dram_tensor("c_id", [128, 128], F32, kind="ExternalInput").ap()
    c_iota = nc.dram_tensor("c_iota", [128, 8], F32, kind="ExternalInput").ap()
    c_io99 = nc.dram_tensor("c_io99", [128, 8], F32, kind="ExternalInput").ap()
    cst = nc.dram_tensor("cst", [128, len(CONST_VALS)], F32, kind="ExternalInput").ap()
    yout = nc.dram_tensor("yout", [TOK, DM], F32, kind="ExternalOutput").ap()

    with ExitStack() as ctx:
        E = ctx.enter_context
        wo_sb = E(nc.sbuf_tensor("wo_sb", [128, 8 * DM], F32))
        gw_sb = E(nc.sbuf_tensor("gw_sb", [128, 8 * 8], F32))
        gaug = E(nc.sbuf_tensor("gaug", [2, 8], F32))
        ob_sb = E(nc.sbuf_tensor("ob_sb", [1, DM], F32))
        onr = E(nc.sbuf_tensor("onr", [1, 128], F32))
        afm_t = E(nc.sbuf_tensor("afm_t", [128, 2 * DM], F32))
        xt_t = E(nc.sbuf_tensor("xt_t", [128, 2 * DM], F32))
        r2tm = E(nc.sbuf_tensor("r2tm", [128, BT * DM], F32))
        r2fm = E(nc.sbuf_tensor("r2fm", [128, 8 * BLK], F32))
        accb = E(nc.sbuf_tensor("accb", [128, BT * DM], F32))
        web = [E(nc.sbuf_tensor(f"web{i}", [128, 8 * DM], F32)) for i in range(2)]
        statsW = E(nc.sbuf_tensor("statsW", [2, 8 * DM], F32))
        statsT = E(nc.sbuf_tensor("statsT", [2, BLK], F32))
        sqb = E(nc.sbuf_tensor("sqb", [128, DM], F32))
        msi = E(nc.sbuf_tensor("msi", [128, 2 * BT], F32))
        sss = E(nc.sbuf_tensor("sss", [128, BT], F32))
        vt2 = E(nc.sbuf_tensor("vt2", [128, BT], F32))
        rsg = E(nc.sbuf_tensor("rsg", [128, BT], F32))
        gsc = E(nc.sbuf_tensor("gsc", [128, 8 * BT], F32))
        comb = E(nc.sbuf_tensor("comb", [128, 8 * BT], F32))
        tk = E(nc.sbuf_tensor("tk", [128, 8 * 6], F32))
        tk1 = E(nc.sbuf_tensor("tk1", [128, 8], F32))
        siota = E(nc.sbuf_tensor("siota", [128, 8], F32))
        sio99 = E(nc.sbuf_tensor("sio99", [128, 8], F32))
        sones = E(nc.sbuf_tensor("sones", [128, 1], F32))
        sid = E(nc.sbuf_tensor("sid", [128, 128], F32))
        scst = E(nc.sbuf_tensor("scst", [128, len(CONST_VALS)], F32))
        PSA = E(nc.psum_tensor("PSA", [128, 2048], F32))
        PSB = E(nc.psum_tensor("PSB", [128, 2048], F32))

        _register_consts(nc, scst[:])

        sc = Sched(nc, ctx)
        s_ld = sc.sem("s_ld", 16)
        s_we = sc.sem("s_we", 16)
        s_t = sc.sem("s_t")
        s_v = sc.sem("s_v")
        s_s = sc.sem("s_s")
        s_o = sc.sem("s_o", 16)

        wr = {}
        rd = {}

        def W(buf):
            return list(rd.get(buf, [])) + list(wr.get(buf, []))

        def R(*bufs):
            out = []
            for b2 in bufs:
                out += list(wr.get(b2, []))
            return out

        def setw(buf, sem):
            wr[buf] = [(sem, sc.cur(sem))]
            rd[buf] = []

        def addr(buf, sem):
            rd.setdefault(buf, [])
            rd[buf] = [x for x in rd[buf] if x[0] != sem] + [(sem, sc.cur(sem))]

        for t, s in ((sones[:], c_od[:]), (onr[:], c_o128[:]), (sid[:], c_id[:]),
                     (siota[:], c_iota[:]), (sio99[:], c_io99[:]), (scst[:], cst[:]), (ob_sb[:], ob[:]),
                     (gaug[1:2, :], gb[:])):
            def mk(t=t, s=s):
                return lambda e: e.dma_start(out=t, in_=s)
            sc.emit("sync", mk(), inc=s_ld)
        for cch in range(8):
            def mk(cch=cch):
                return lambda e: e.dma_start(out=wo_sb[:, cch * DM:cch * DM + DM],
                                             in_=wo[cch * 128:cch * 128 + 128, :])
            sc.emit("sync", mk(), inc=s_ld)
        for cch in range(8):
            def mk(cch=cch):
                return lambda e: e.dma_start(out=gw_sb[:, cch * 8:cch * 8 + 8],
                                             in_=gw[cch * 128:cch * 128 + 128, :])
            sc.emit("sync", mk(), inc=s_ld)
        for e_ in range(8):
            def mk(e_=e_):
                return lambda e: e.dma_start(out=statsW[1:2, e_ * DM:e_ * DM + DM],
                                             in_=eb[e_])
            sc.emit("sync", mk(), inc=s_ld)
        BAR = [(s_ld, sc.cur(s_ld))]
        p_g = PSB[0:1, 0:8]
        for cch in range(8):
            def mk(cch=cch):
                return lambda e: e.matmul(p_g, sones[:], gw_sb[:, cch * 8:cch * 8 + 8],
                                          start=(cch == 0), stop=(cch == 7))
            sc.emit("tensor", mk(), waits=BAR, inc=s_t)
        setw("PB45", s_t)
        sc.emit("vector", lambda e: e.tensor_scalar(gaug[0:1, :], p_g, -float(DM),
                                                    None, op0=ALU.mult),
                waits=R("PB45") + BAR, inc=s_v)
        setw("gaug0", s_v)
        addr("PB45", s_v)

        def we_load(e_, buf):
            for cch in range(8):
                def mk(e_=e_, cch=cch, buf=buf):
                    return lambda e: e.dma_start(out=web[buf][:, cch * DM:cch * DM + DM],
                                                 in_=ew[e_, cch * 128:cch * 128 + 128, :])
                w0 = ([(s_we, sc.cur(s_we))] + W(f"web{buf}")) if cch == 0 else []
                sc.emit("sync", mk(), waits=w0, inc=s_we)
            setw(f"web{buf}", s_we)

        we_load(0, 0)
        nstream = 1

        for b in range(NB):
            t0 = b * BLK
            for tt in range(BT):
                slot = tt % 2
                for cch in range(8):
                    def mk(tt=tt, cch=cch, t0=t0, slot=slot):
                        return lambda e: e.dma_start(
                            out=afm_t[:, slot * DM + cch * 128:slot * DM + cch * 128 + 128],
                            in_=afm[cch * 128:cch * 128 + 128,
                                    t0 + tt * 128:t0 + tt * 128 + 128])
                    w0 = ([(s_ld, sc.cur(s_ld))] + W(f"afm{slot}")) if cch == 0 else []
                    sc.emit("sync", mk(), waits=w0, inc=s_ld)
                def mkx(tt=tt, t0=t0, slot=slot):
                    return lambda e: e.dma_start(
                        out=xt_t[:, slot * DM:slot * DM + DM],
                        in_=xtm[t0 + tt * 128:t0 + tt * 128 + 128, :])
                sc.emit("sync", mkx(), waits=W(f"xt{slot}"), inc=s_ld)
                setw(f"afm{slot}", s_ld)
                setw(f"xt{slot}", s_ld)
                for half in range(2):
                    p_y = PSA[:, half * 512:half * 512 + 512]
                    for cch in range(8):
                        def mk(half=half, cch=cch, p_y=p_y, slot=slot):
                            return lambda e: e.matmul(
                                p_y,
                                afm_t[:, slot * DM + cch * 128:slot * DM + cch * 128 + 128],
                                wo_sb[:, cch * DM + half * 512:cch * DM + half * 512 + 512],
                                start=(cch == 0), stop=False)
                        ww = (R(f"afm{slot}") + BAR + W("PSA01")) if cch == 0 else []
                        sc.emit("tensor", mk(), waits=ww, inc=s_t)
                    def mka(half=half, p_y=p_y):
                        return lambda e: e.matmul(
                            p_y, onr[:], ob_sb[:, half * 512:half * 512 + 512],
                            start=False, stop=True)
                    sc.emit("tensor", mka(), inc=s_t)
                setw("PSA01", s_t)
                addr(f"afm{slot}", s_t)
                def mkev(tt=tt, slot=slot):
                    return lambda e: e.tensor_add(r2tm[:, tt * DM:tt * DM + DM],
                                                  xt_t[:, slot * DM:slot * DM + DM],
                                                  PSA[:, 0:1024])
                sc.emit("vector", mkev(), waits=R("PSA01", f"xt{slot}") +
                        (W("r2tm") if tt == 0 else []), inc=s_v)
                addr("PSA01", s_v)
                addr(f"xt{slot}", s_v)
            setw("r2tm", s_v)

            for tt in range(BT):
                def mkmu(tt=tt):
                    return lambda e: e.tensor_reduce(
                        msi[:, 2 * tt:2 * tt + 1], r2tm[:, tt * DM:tt * DM + DM],
                        axis=AX.X, op=ALU.add)
                sc.emit("vector", mkmu(), waits=R("r2tm") + (W("msi") if tt == 0 else []),
                        inc=s_v)
                def mksq(tt=tt):
                    return lambda e: e.tensor_mul(sqb[:], r2tm[:, tt * DM:tt * DM + DM],
                                                  r2tm[:, tt * DM:tt * DM + DM])
                sc.emit("vector", mksq(), waits=R("r2tm") + W("sqb"), inc=s_v)
                setw("sqb", s_v)
                def mkss(tt=tt):
                    return lambda e: e.tensor_reduce(sss[:, tt:tt + 1], sqb[:],
                                                     axis=AX.X, op=ALU.add)
                sc.emit("vector", mkss(), waits=R("sqb") + (W("sss") if tt == 0 else []),
                        inc=s_v)
                addr("sqb", s_v)
            mu_v = msi[:].rearrange("p (c t) -> p c t", t=2)
            sc.emit("vector", lambda e: e.tensor_scalar(mu_v[:, :, 0], mu_v[:, :, 0],
                                                        1.0 / DM, None, op0=ALU.mult),
                    waits=[(s_v, sc.cur(s_v))], inc=s_v)
            sc.emit("vector", lambda e: e.tensor_scalar(sss[:], sss[:], 1.0 / DM,
                                                        None, op0=ALU.mult),
                    waits=[(s_v, sc.cur(s_v))], inc=s_v)
            sc.emit("vector", lambda e: e.tensor_mul(vt2[:], mu_v[:, :, 0], mu_v[:, :, 0]),
                    waits=[(s_v, sc.cur(s_v))] + W("vt2"), inc=s_v)
            setw("vt2", s_v)
            sc.emit("vector", lambda e: e.tensor_sub(sss[:], sss[:], vt2[:]),
                    waits=[(s_v, sc.cur(s_v))], inc=s_v)
            setw("sss", s_v)
            sc.emit("scalar", lambda e: e.activation(sss[:], sss[:], AF.Ln,
                                                     bias=EPS_LN, scale=1.0),
                    waits=BAR + R("sss") + [(s_s, sc.cur(s_s))], inc=s_s)
            setw("sss", s_s)
            sc.emit("scalar", lambda e: e.activation(mu_v[:, :, 1], sss[:], AF.Exp,
                                                     scale=0.5),
                    waits=R("sss") + [(s_s, sc.cur(s_s))], inc=s_s)
            setw("msi_sig", s_s)
            sc.emit("scalar", lambda e: e.activation(rsg[:], sss[:], AF.Exp, scale=-0.5),
                    waits=R("sss") + W("rsg") + [(s_s, sc.cur(s_s))], inc=s_s)
            setw("rsg", s_s)
            addr("sss", s_s)
            p_t2 = PSB[0:2, 512:640]
            for tt in range(BT):
                def mk(tt=tt):
                    return lambda e: e.transpose(p_t2, msi[:, 2 * tt:2 * tt + 2], sid[:])
                sc.emit("tensor", mk(), waits=BAR + R("msi_sig") + [(s_v, sc.cur(s_v))] +
                        W("PB5"), inc=s_t)
                setw("PB5", s_t)
                def mkc(tt=tt):
                    return lambda e: e.tensor_copy(statsT[0:2, tt * 128:tt * 128 + 128],
                                                   p_t2)
                sc.emit("vector", mkc(), waits=R("PB5") + (W("statsT") if tt == 0 else []),
                        inc=s_v)
                addr("PB5", s_v)
            setw("statsT", s_v)
            addr("msi_sig", s_t)

            p_tr = PSB[:, 1024:1152]
            for tt in range(BT):
                for cch in range(8):
                    def mk(tt=tt, cch=cch):
                        return lambda e: e.transpose(
                            p_tr, r2tm[:, tt * DM + cch * 128:tt * DM + cch * 128 + 128],
                            sid[:])
                    sc.emit("tensor", mk(), waits=BAR + R("r2tm") + W("PB6"), inc=s_t)
                    setw("PB6", s_t)
                    def mkc(tt=tt, cch=cch):
                        return lambda e: e.tensor_copy(
                            r2fm[:, cch * BLK + tt * 128:cch * BLK + tt * 128 + 128], p_tr)
                    sc.emit("vector", mkc(), waits=R("PB6") +
                            (W("r2fm") if (tt == 0 and cch == 0) else []), inc=s_v)
                    addr("PB6", s_v)
            setw("r2fm", s_v)
            addr("r2tm", s_t)

            p_gs = PSB[0:128, 0:8]
            for tt in range(BT):
                for cch in range(8):
                    def mk(tt=tt, cch=cch):
                        return lambda e: e.matmul(
                            p_gs, r2fm[:, cch * BLK + tt * 128:cch * BLK + tt * 128 + 128],
                            gw_sb[:, cch * 8:cch * 8 + 8],
                            start=(cch == 0), stop=False)
                    ww = (R("r2fm", "gaug0") + W("PB45")) if cch == 0 else []
                    sc.emit("tensor", mk(), waits=ww, inc=s_t)
                def mka(tt=tt):
                    return lambda e: e.matmul(p_gs, statsT[0:2, tt * 128:tt * 128 + 128],
                                              gaug[0:2, :], start=False, stop=True)
                sc.emit("tensor", mka(), waits=R("statsT"), inc=s_t)
                setw("PB45", s_t)
                def mkev(tt=tt):
                    return lambda e: e.activation(gsc[:, tt * 8:tt * 8 + 8], p_gs, AF.Copy,
                                                  scale=rsg[:, tt:tt + 1])
                sc.emit("scalar", mkev(), waits=R("PB45", "rsg") + [(s_s, sc.cur(s_s))] +
                        (W("gsc") if tt == 0 else []), inc=s_s)
                addr("PB45", s_s)
            setw("gsc", s_s)

            for tt in range(BT):
                g0 = gsc[:, tt * 8:tt * 8 + 8]
                t_s, t_p = tk[:, 0:8], tk[:, 8:16]
                t_eq, t_cd = tk[:, 16:24], tk[:, 24:32]
                t_sel, t_wm = tk[:, 32:40], tk[:, 40:48]
                c_m, c_i = tk1[:, 0:1], tk1[:, 1:2]
                c_z, c_t = tk1[:, 2:3], tk1[:, 3:4]
                sc.emit("vector", lambda e, g0=g0, t_s=t_s: e.tensor_scalar(
                        t_s, g0, GEPS, None, op0=ALU.max),
                        waits=R("gsc") + W("tk") + [(s_v, sc.cur(s_v))], inc=s_v)
                sc.emit("scalar", lambda e, t_s=t_s, t_p=t_p: e.activation(t_p, t_s, AF.Exp),
                        waits=[(s_v, sc.cur(s_v)), (s_s, sc.cur(s_s))], inc=s_s)
                sc.emit("vector", lambda e, t_p=t_p, c_z=c_z: e.tensor_reduce(
                        c_z, t_p, axis=AX.X, op=ALU.add),
                        waits=[(s_s, sc.cur(s_s)), (s_v, sc.cur(s_v))] + W("tk1"), inc=s_v)
                sc.emit("vector", lambda e, c_z=c_z: e.reciprocal(c_z, c_z),
                        waits=[(s_v, sc.cur(s_v))], inc=s_v)
                sc.emit("vector", lambda e, t_p=t_p, c_z=c_z: e.tensor_scalar(
                        t_p, t_p, c_z, None, op0=ALU.mult),
                        waits=[(s_v, sc.cur(s_v))], inc=s_v)
                sc.emit("vector", lambda e, t_p=t_p, c_m=c_m: e.tensor_reduce(
                        c_m, t_p, axis=AX.X, op=ALU.max),
                        waits=[(s_v, sc.cur(s_v))], inc=s_v)
                sc.emit("vector", lambda e, t_p=t_p, c_m=c_m, t_eq=t_eq: e.tensor_scalar(
                        t_eq, t_p, c_m, None, op0=ALU.is_equal),
                        waits=[(s_v, sc.cur(s_v))], inc=s_v)
                sc.emit("vector", lambda e, t_cd=t_cd, t_eq=t_eq: e.tensor_mul(
                        t_cd, t_eq, sio99[:]),
                        waits=BAR + [(s_v, sc.cur(s_v))], inc=s_v)
                sc.emit("vector", lambda e, t_cd=t_cd: e.tensor_scalar(
                        t_cd, t_cd, 99.0, None, op0=ALU.add),
                        waits=[(s_v, sc.cur(s_v))], inc=s_v)
                sc.emit("vector", lambda e, t_cd=t_cd, c_i=c_i: e.tensor_reduce(
                        c_i, t_cd, axis=AX.X, op=ALU.min),
                        waits=[(s_v, sc.cur(s_v))], inc=s_v)
                sc.emit("vector", lambda e, c_i=c_i, t_sel=t_sel: e.tensor_scalar(
                        t_sel, siota[:], c_i, None, op0=ALU.is_equal),
                        waits=[(s_v, sc.cur(s_v))], inc=s_v)
                sc.emit("vector", lambda e, t_sel=t_sel, t_p=t_p, t_eq=t_eq:
                        e.scalar_tensor_tensor(t_eq, t_sel, -1e30, t_p,
                                               op0=ALU.mult, op1=ALU.add),
                        waits=[(s_v, sc.cur(s_v))], inc=s_v)
                sc.emit("vector", lambda e, t_eq=t_eq, c_m=c_m: e.tensor_reduce(
                        c_m, t_eq, axis=AX.X, op=ALU.max),
                        waits=[(s_v, sc.cur(s_v))], inc=s_v)
                sc.emit("vector", lambda e, t_eq=t_eq, c_m=c_m: e.tensor_scalar(
                        t_eq, t_eq, c_m, None, op0=ALU.is_equal),
                        waits=[(s_v, sc.cur(s_v))], inc=s_v)
                sc.emit("vector", lambda e, t_cd=t_cd, t_eq=t_eq: e.tensor_mul(
                        t_cd, t_eq, sio99[:]),
                        waits=[(s_v, sc.cur(s_v))], inc=s_v)
                sc.emit("vector", lambda e, t_cd=t_cd: e.tensor_scalar(
                        t_cd, t_cd, 99.0, None, op0=ALU.add),
                        waits=[(s_v, sc.cur(s_v))], inc=s_v)
                sc.emit("vector", lambda e, t_cd=t_cd, c_i=c_i: e.tensor_reduce(
                        c_i, t_cd, axis=AX.X, op=ALU.min),
                        waits=[(s_v, sc.cur(s_v))], inc=s_v)
                sc.emit("vector", lambda e, c_i=c_i, t_eq=t_eq: e.tensor_scalar(
                        t_eq, siota[:], c_i, None, op0=ALU.is_equal),
                        waits=[(s_v, sc.cur(s_v))], inc=s_v)
                sc.emit("vector", lambda e, t_sel=t_sel, t_eq=t_eq: e.tensor_add(
                        t_sel, t_sel, t_eq),
                        waits=[(s_v, sc.cur(s_v))], inc=s_v)
                sc.emit("vector", lambda e, t_p=t_p, t_wm=t_wm: e.tensor_scalar(
                        t_wm, t_p, WTHR, None, op0=ALU.max),
                        waits=[(s_v, sc.cur(s_v))], inc=s_v)
                sc.emit("vector", lambda e, t_wm=t_wm, t_sel=t_sel: e.tensor_mul(
                        t_wm, t_wm, t_sel),
                        waits=[(s_v, sc.cur(s_v))], inc=s_v)
                sc.emit("vector", lambda e, t_wm=t_wm, c_t=c_t: e.tensor_reduce(
                        c_t, t_wm, axis=AX.X, op=ALU.add),
                        waits=[(s_v, sc.cur(s_v))], inc=s_v)
                sc.emit("vector", lambda e, c_t=c_t: e.tensor_scalar(
                        c_t, c_t, 2.0, None, op0=ALU.mult),
                        waits=[(s_v, sc.cur(s_v))], inc=s_v)
                sc.emit("vector", lambda e, c_t=c_t: e.reciprocal(c_t, c_t),
                        waits=[(s_v, sc.cur(s_v))], inc=s_v)
                sc.emit("vector", lambda e, c_t=c_t, tt=tt: e.tensor_mul(
                        c_t, c_t, rsg[:, tt:tt + 1]),
                        waits=[(s_v, sc.cur(s_v))] + R("rsg"), inc=s_v)
                sc.emit("vector", lambda e, t_wm=t_wm, c_t=c_t, tt=tt: e.tensor_scalar(
                        comb[:, tt * 8:tt * 8 + 8], t_wm, c_t, None, op0=ALU.mult),
                        waits=[(s_v, sc.cur(s_v))] + (W("comb") if tt == 0 else []),
                        inc=s_v)
            setw("comb", s_v)
            addr("gsc", s_v)
            addr("rsg", s_v)

            sc.emit("vector", lambda e: e.memset(accb[:], 0.0),
                    waits=W("accb") + [(s_v, sc.cur(s_v))], inc=s_v)
            setw("accb", s_v)
            for e_ in range(8):
                buf = nstream % 2 if False else (b * 8 + e_) % 2
                webuf = web[buf]
                wname = f"web{buf}"
                if b == 0:
                    for half in range(2):
                        p_sw = PSA[0:1, 1024 + half * 512:1024 + half * 512 + 512]
                        for cch in range(8):
                            def mk(cch=cch, half=half, webuf=webuf):
                                return lambda e: e.matmul(
                                    p_sw if False else PSA[0:1, 1024 + half * 512:1536 + half * 512],
                                    sones[:],
                                    webuf[:, cch * DM + half * 512:cch * DM + half * 512 + 512],
                                    start=(cch == 0), stop=(cch == 7))
                            ww = (R(wname) + W("PSA23")) if (cch == 0 and half == 0) else []
                            sc.emit("tensor", mk(), waits=ww, inc=s_t)
                    setw("PSA23", s_t)
                    def mksw(e_=e_):
                        return lambda e: e.tensor_scalar(
                            statsW[0:1, e_ * DM:e_ * DM + DM], PSA[0:1, 1024:2048],
                            -float(DM), None, op0=ALU.mult)
                    sc.emit("vector", mksw(), waits=R("PSA23") +
                            (W("statsW0") if e_ == 0 else []), inc=s_v)
                    addr("PSA23", s_v)
                    wr[f"sW{e_}"] = [(s_v, sc.cur(s_v))]
                    if e_ == 7:
                        setw("statsW0", s_v)
                if nstream < NB * 8:
                    nxt = nstream % 8
                    we_load(nxt, (nstream) % 2 if False else nstream % 2)
                    nstream += 1
                for tt in range(BT):
                    for half in range(2):
                        p_y = PSA[:, half * 512:half * 512 + 512]
                        for cch in range(8):
                            def mk(tt=tt, half=half, cch=cch, webuf=webuf, p_y=p_y):
                                return lambda e: e.matmul(
                                    p_y,
                                    r2fm[:, cch * BLK + tt * 128:cch * BLK + tt * 128 + 128],
                                    webuf[:, cch * DM + half * 512:cch * DM + half * 512 + 512],
                                    start=(cch == 0), stop=False)
                            ww = (R("r2fm", wname, f"sW{e_}") + W("PSA01")) if (cch == 0 and half == 0) else []
                            sc.emit("tensor", mk(), waits=ww, inc=s_t)
                        def mka(tt=tt, half=half, e_=e_, p_y=p_y):
                            return lambda e: e.matmul(
                                p_y, statsT[0:2, tt * 128:tt * 128 + 128],
                                statsW[0:2, e_ * DM + half * 512:e_ * DM + half * 512 + 512],
                                start=False, stop=True)
                        sc.emit("tensor", mka(), waits=R("statsT") if (tt == 0 and half == 0) else [],
                                inc=s_t)
                    setw("PSA01", s_t)
                    def mkcmb(tt=tt, e_=e_):
                        return lambda e: e.scalar_tensor_tensor(
                            accb[:, tt * DM:tt * DM + DM], PSA[:, 0:1024],
                            comb[:, tt * 8 + e_:tt * 8 + e_ + 1],
                            accb[:, tt * DM:tt * DM + DM],
                            op0=ALU.mult, op1=ALU.add)
                    sc.emit("vector", mkcmb(), waits=R("PSA01", "comb", "accb"), inc=s_v)
                    addr("PSA01", s_v)
                    wr["accb"] = [(s_v, sc.cur(s_v))]
                addr(wname, s_t)
            addr("r2fm", s_t)
            addr("statsT", s_t)
            addr("comb", s_v)

            for tt in range(BT):
                def mkf(tt=tt):
                    return lambda e: e.tensor_add(accb[:, tt * DM:tt * DM + DM],
                                                  accb[:, tt * DM:tt * DM + DM],
                                                  r2tm[:, tt * DM:tt * DM + DM])
                sc.emit("vector", mkf(), waits=R("accb", "r2tm") + [(s_v, sc.cur(s_v))],
                        inc=s_v)
                wr["accb"] = [(s_v, sc.cur(s_v))]
                def mko(tt=tt, t0=t0):
                    return lambda e: e.dma_start(
                        out=yout[t0 + tt * 128:t0 + tt * 128 + 128, :],
                        in_=accb[:, tt * DM:tt * DM + DM])
                sc.emit("gpsimd", mko(), waits=[(s_o, sc.cur(s_o))] + R("accb"), inc=s_o)
                addr("accb", s_o)
            addr("r2tm", s_v)

        sc.emit("gpsimd", lambda e: e.nop(), waits=[(s_o, sc.cur(s_o))])

        block = E(nc.Block())
        sc.finalize(block)
    return nc


def ref_tail_np(afm_b, x_b, out_w, out_b, gate_W, gate_b, expert_W, expert_b):
    a = afm_b.astype(np.float64).T
    x = x_b.astype(np.float64)
    r2 = x + a @ out_w.astype(np.float64).T + out_b.astype(np.float64)
    mu = r2.mean(1, keepdims=True)
    sig = np.sqrt(r2.var(1) + EPS_LN)[:, None]
    ln = (r2 - mu) / sig
    scores = np.maximum(ln @ gate_W.astype(np.float64) + gate_b.astype(np.float64), GEPS)
    e = np.exp(scores - scores.max(1, keepdims=True))
    p = e / e.sum(1, keepdims=True)
    T = p.shape[0]
    comb = np.zeros_like(p)
    for t in range(T):
        idx = np.argsort(-p[t], kind="stable")[:2]
        w_ = np.maximum(p[t], WTHR)
        tot = w_[idx].sum()
        comb[t, idx] = w_[idx] / tot
    y = np.einsum("td,edf->tef", ln, expert_W.astype(np.float64)) + expert_b.astype(np.float64)
    out = np.einsum("tef,te->tf", y, comb) / 2.0
    return r2 + out

def ref_attn_np(x, wqk_slices, wv_slices, bqk_s, bv_s):
    xs = x.astype(np.float64)
    mu = xs.mean(1, keepdims=True)
    var = xs.var(1)
    sig = np.sqrt(var + EPS_LN)
    ln = (xs - mu) / sig[:, None]
    outs = []
    for wqk_h, wv_h, bqk_h, bv_h in zip(wqk_slices, wv_slices, bqk_s, bv_s):
        qk = ln @ wqk_h.astype(np.float64) + bqk_h.astype(np.float64)
        q, k = qk[:, 0:64], qk[:, 64:128]
        v = ln @ wv_h.astype(np.float64) + bv_h.astype(np.float64)
        qn = q / np.maximum(np.linalg.norm(q, axis=1, keepdims=True), 1e-12)
        kn = k / np.maximum(np.linalg.norm(k, axis=1, keepdims=True), 1e-12)
        c = np.clip(qn @ kn.T, -1 + 1e-7, 1 - 1e-7)
        g = np.arccos(c)
        s = (1.0 + g) ** POW
        col = s.sum(0) ** -0.5
        num = (s * col[None, :]) @ v
        den = (s * col[None, :]).sum(1, keepdims=True)
        outs.append((num / den).T)
    return outs



def _attn_inputs_for_core(core, x, qkv_w, qkv_b, S=S_FULL, hpc=HPC):
    b = core // (NCORES // B)
    h0 = (core % (NCORES // B)) * hpc
    xfm = np.ascontiguousarray(x[b].T)
    wqk = np.empty((hpc, DM, 128), np.float32)
    wv = np.empty((hpc, DM, 64), np.float32)
    bqk = np.empty((hpc, 1, 128), np.float32)
    bv = np.empty((hpc, 1, 64), np.float32)
    for i in range(hpc):
        h = h0 + i
        wq = qkv_w[h * 64:(h + 1) * 64, :]
        wk = qkv_w[DM + h * 64:DM + (h + 1) * 64, :]
        wvh = qkv_w[2 * DM + h * 64:2 * DM + (h + 1) * 64, :]
        wqk[i, :, 0:64] = wq.T
        wqk[i, :, 64:128] = wk.T
        wv[i] = wvh.T
        bqk[i, 0, 0:64] = qkv_b[h * 64:(h + 1) * 64]
        bqk[i, 0, 64:128] = qkv_b[DM + h * 64:DM + (h + 1) * 64]
        bv[i, 0] = qkv_b[2 * DM + h * 64:2 * DM + (h + 1) * 64]
    c_od = np.full((128, 1), 1.0 / DM, np.float32)
    c_sel = np.zeros((128, 2), np.float32)
    c_sel[0:64, 0] = 1.0
    c_sel[64:128, 1] = 1.0
    c_bc2 = np.zeros((2, 128), np.float32)
    c_bc2[0, 0:64] = 1.0
    c_bc2[1, 64:128] = 1.0
    c_o64 = np.ones((1, 64), np.float32)
    c_id = np.eye(128, dtype=np.float32)
    return {"xfm": xfm, "wqk": wqk, "wv": wv, "bqk": bqk, "bv": bv,
            "c_od": c_od, "c_sel": c_sel, "c_bc2": c_bc2, "c_o64": c_o64,
            "c_id": c_id, "cst": _consts_np()}


def _tail_inputs_for_core(core, afm, x, out_w, out_b, gate_W, gate_b,
                          expert_W, expert_b, TOK=S_FULL * B // NCORES):
    b = core // (NCORES // B)
    t0 = (core % (NCORES // B)) * TOK
    return {
        "afm": np.ascontiguousarray(afm[b][:, t0:t0 + TOK]),
        "xtm": np.ascontiguousarray(x[b][t0:t0 + TOK, :]),
        "wo": np.ascontiguousarray(out_w.T),
        "ob": out_b[None, :].astype(np.float32),
        "gw": gate_W,
        "gb": gate_b[None, :].astype(np.float32),
        "ew": expert_W,
        "eb": np.ascontiguousarray(expert_b[:, None, :]),
        "c_od": np.full((128, 1), 1.0 / DM, np.float32),
        "c_o128": np.ones((1, 128), np.float32),
        "c_id": np.eye(128, dtype=np.float32),
        "c_iota": np.broadcast_to(np.arange(8, dtype=np.float32), (128, 8)).copy(),
        "c_io99": np.broadcast_to(np.arange(8, dtype=np.float32) - 99.0, (128, 8)).copy(),
        "cst": _consts_np(),
    }


def kernel(L0_attn_x, L0_attn_qkv_w, L0_attn_qkv_b, L0_attn_out_w, L0_attn_out_b,
           L0_ffn_gate_W, L0_ffn_gate_b, L0_ffn_expert_W, L0_ffn_expert_b):
    x = np.ascontiguousarray(np.asarray(L0_attn_x, np.float32))
    qkv_w = np.ascontiguousarray(np.asarray(L0_attn_qkv_w, np.float32))
    qkv_b = np.ascontiguousarray(np.asarray(L0_attn_qkv_b, np.float32))
    out_w = np.ascontiguousarray(np.asarray(L0_attn_out_w, np.float32))
    out_b = np.ascontiguousarray(np.asarray(L0_attn_out_b, np.float32))
    gate_W = np.ascontiguousarray(np.asarray(L0_ffn_gate_W, np.float32))
    gate_b = np.ascontiguousarray(np.asarray(L0_ffn_gate_b, np.float32))
    expert_W = np.ascontiguousarray(np.asarray(L0_ffn_expert_W, np.float32))
    expert_b = np.ascontiguousarray(np.asarray(L0_ffn_expert_b, np.float32))

    nc1 = build_attn()
    in_maps = [_attn_inputs_for_core(c, x, qkv_w, qkv_b) for c in range(NCORES)]
    res1 = run_bass_kernel_spmd(nc1, in_maps, core_ids=list(range(NCORES)))
    afm = np.empty((B, DM, S_FULL), np.float32)
    for c in range(NCORES):
        b = c // (NCORES // B)
        h0 = (c % (NCORES // B)) * HPC
        a = res1.results[c]["attn"]
        for i in range(HPC):
            afm[b, (h0 + i) * 64:(h0 + i + 1) * 64, :] = a[i]

    nc2 = build_tail()
    in_maps2 = [_tail_inputs_for_core(c, afm, x, out_w, out_b, gate_W, gate_b,
                                      expert_W, expert_b) for c in range(NCORES)]
    res2 = run_bass_kernel_spmd(nc2, in_maps2, core_ids=list(range(NCORES)))
    TOK = S_FULL * B // NCORES
    out = np.empty((B, S_FULL, DM), np.float32)
    for c in range(NCORES):
        b = c // (NCORES // B)
        t0 = (c % (NCORES // B)) * TOK
        out[b, t0:t0 + TOK, :] = res2.results[c]["yout"]
    return out
